# version 08
# baseline (speedup 1.0000x reference)
import numpy as np
import ml_dtypes
import concourse.bass as bass
import concourse.mybir as mybir
from concourse.alu_op_type import AluOpType
from concourse.bass_utils import run_bass_kernel_spmd

F32 = mybir.dt.float32
BF16 = mybir.dt.bfloat16
AF = mybir.ActivationFunctionType
OP = AluOpType

H, DK, DV, M, KW = 16, 128, 128, 128, 4
D_MODEL = 2048
T_CORE = 1024
C = 64
NPAIR = 8
NTG = 8
HPG = 4
NHG = 4
NCH = 16
SCALE = DK ** -0.5
LNSC = float(np.log(SCALE))
EPS = 1e-5
RG = [[0, 1, 2, 3], [4, 5, 6, 7]]
MOE_E = 16
MOE_IC = 11


class Prog:
    ENGINES = ("tensor", "vector", "scalar", "gpsimd", "sync")
    TINY = {"stat", "fws", "combine", "dbgs",
            "vw0", "vw1", "vw2", "vw3", "ev0", "ev1", "ev2", "ev3"}

    def __init__(self, nc):
        self.nc = nc
        self.ops = {e: [] for e in self.ENGINES}
        self.res = {}
        self.dmatags = {}
        self.final_tags = []
        self.lastw = {e: set() for e in self.ENGINES}

    def _st(self, name):
        return self.res.setdefault(name, {"w": None, "rs": []})

    def op(self, engine, fn, reads=(), writes=(), dma_tag=None, extra_deps=()):
        evs = list(extra_deps)
        for r in reads:
            st = self.res.get(r)
            if st is not None and st["w"] is not None:
                evs.append(st["w"])
        for w in writes:
            st = self.res.get(w)
            if st is not None:
                if st["w"] is not None:
                    evs.append(st["w"])
                evs.extend(st["rs"])
        if dma_tag is not None:
            cnt = self.dmatags.get(dma_tag, 0) + 1
            self.dmatags[dma_tag] = cnt
            myev = ("D", dma_tag, cnt)
        else:
            myev = ("E", engine, len(self.ops[engine]))
        best = {}
        for ev in evs:
            if ev[0] == "E" and ev[1] == engine:
                continue
            k = (ev[0], ev[1])
            if k not in best or best[k] < ev[2]:
                best[k] = ev[2]
        deps = [(k[0], k[1], v) for k, v in best.items()]
        if engine in ("vector", "scalar") and dma_tag is None:
            hazard = (set(reads) | set(writes)) & self.lastw[engine] & self.TINY
            if hazard:
                self.ops[engine].append({"fn": lambda eng: eng.drain(),
                                         "deps": [], "ev": None, "need": False})
                if myev[0] == "E":
                    myev = ("E", engine, len(self.ops[engine]))
        self.lastw[engine] = set(writes)
        self.ops[engine].append({"fn": fn, "deps": deps, "ev": myev, "need": False})
        for r in reads:
            self._st(r)["rs"].append(myev)
        for w in writes:
            st = self._st(w)
            st["w"] = myev
            st["rs"] = []
        return myev

    def inherit(self, news, olds):
        for new in news:
            st = self._st(new)
            for o in olds:
                os_ = self.res.get(o)
                if os_ is None:
                    continue
                if os_["w"] is not None:
                    st["rs"].append(os_["w"])
                st["rs"].extend(os_["rs"])

    def emit(self):
        nc = self.nc
        for e in self.ENGINES:
            for o in self.ops[e]:
                for d in o["deps"]:
                    if d[0] == "E":
                        self.ops[d[1]][d[2]]["need"] = True
        rank = {}
        for e in self.ENGINES:
            r = 0
            for i, o in enumerate(self.ops[e]):
                if o["need"]:
                    r += 1
                rank[(e, i)] = r
        import contextlib
        stack = contextlib.ExitStack()
        esem = {e: stack.enter_context(nc.semaphore(f"s_{e}")) for e in self.ENGINES}
        dsem = {t: stack.enter_context(nc.semaphore(f"d_{i}"))
                for i, t in enumerate(self.dmatags)}
        ops, ftags, dtags = self.ops, self.final_tags, self.dmatags
        with stack:
            with nc.Block() as block:
                def make_section(e):
                    def section(eng):
                        hw = {}
                        for o in ops[e]:
                            for d in o["deps"]:
                                if d[0] == "E":
                                    sem, val = esem[d[1]], rank[(d[1], d[2])]
                                else:
                                    sem, val = dsem[d[1]], 16 * d[2]
                                key = (d[0], d[1])
                                if hw.get(key, -1) >= val:
                                    continue
                                eng.wait_ge(sem, val)
                                hw[key] = val
                            inst = o["fn"](eng)
                            if o["ev"] is None:
                                continue
                            if o["ev"][0] == "D":
                                inst.then_inc(dsem[o["ev"][1]], 16)
                            elif o["need"]:
                                inst.then_inc(esem[e], 1)
                        if e == "sync":
                            for t in ftags:
                                if t in dtags:
                                    eng.wait_ge(dsem[t], 16 * dtags[t])
                    return section
                for e in self.ENGINES:
                    getattr(block, e)(make_section(e))

    def dma(self, queue, out, in_, tag, reads=(), writes=(), extra_deps=()):
        return self.op(queue, lambda eng: eng.dma_start(out=out, in_=in_),
                       reads=reads, writes=writes, dma_tag=tag, extra_deps=extra_deps)

    def mm(self, out, lhsT, rhs, start, stop, reads=(), writes=()):
        return self.op("tensor",
                       lambda eng: eng.matmul(out, lhsT, rhs, start=start, stop=stop),
                       reads=reads, writes=writes)

    def transp(self, out, in_, ident, reads=(), writes=()):
        return self.op("tensor", lambda eng: eng.transpose(out, in_, ident),
                       reads=reads, writes=writes)

    def tt(self, engine, out, a, b, alu, reads=(), writes=()):
        return self.op(engine, lambda eng: eng.tensor_tensor(out, a, b, alu),
                       reads=reads, writes=writes)

    def ts(self, engine, out, a, s1, op0, s2=None, op1=None, reads=(), writes=()):
        if s2 is None:
            return self.op(engine, lambda eng: eng.tensor_scalar(out, a, s1, None, op0),
                           reads=reads, writes=writes)
        return self.op(engine, lambda eng: eng.tensor_scalar(out, a, s1, s2, op0, op1),
                       reads=reads, writes=writes)

    def stt(self, engine, out, a, s, b, op0, op1, reads=(), writes=()):
        return self.op(engine,
                       lambda eng: eng.scalar_tensor_tensor(out, a, s, b, op0, op1),
                       reads=reads, writes=writes)

    def act(self, out, in_, func, bias=0.0, scale=1.0, accum=None, reads=(), writes=()):
        if accum is None:
            return self.op("scalar",
                           lambda eng: eng.activation(out, in_, func, bias=bias,
                                                      scale=scale),
                           reads=reads, writes=writes)
        return self.op("scalar",
                       lambda eng: eng.activation(out, in_, func, bias=bias, scale=scale,
                                                  accum_out=accum),
                       reads=reads, writes=writes)

    def red(self, out, in_, alu, negate=False, reads=(), writes=()):
        return self.op("vector",
                       lambda eng: eng.tensor_reduce(out, in_, mybir.AxisListType.X, alu,
                                                     negate=negate),
                       reads=reads, writes=writes)

    def cp(self, engine, out, in_, reads=(), writes=()):
        if engine == "scalar":
            return self.op(engine, lambda eng: eng.copy(out, in_),
                           reads=reads, writes=writes)
        return self.op(engine, lambda eng: eng.tensor_copy(out, in_),
                       reads=reads, writes=writes)

    def recip(self, ap, reads=(), writes=()):
        return self.op("vector", lambda eng: eng.reciprocal(ap, ap),
                       reads=reads, writes=writes)


INTERLEAVE = True
B_LEVEL = 9
LOWCOLS = False
SKIP_SCAN = False
SKIP_AG = False
SKIP_D = False


def drive(gens):
    if not INTERLEAVE:
        for g in gens:
            for _ in g:
                pass
        return
    alive = list(gens)
    while alive:
        nxt = []
        for g in alive:
            try:
                next(g)
                nxt.append(g)
            except StopIteration:
                pass
        alive = nxt


def build_program(taps=()):
    nc = bass.Bass("TRN2", target_bir_lowering=False, debug=False)
    taps = set(taps)

    dx = nc.dram_tensor("xloc", [T_CORE, D_MODEL], F32, kind="ExternalInput").ap()
    dhalo = nc.dram_tensor("xhalo", [3, D_MODEL], F32, kind="ExternalInput").ap()
    wd = {}
    for nm, shp in [("Wq", [D_MODEL, D_MODEL]), ("Wk", [D_MODEL, D_MODEL]),
                    ("Wv", [D_MODEL, D_MODEL]), ("Ww", [D_MODEL, D_MODEL]),
                    ("cq", [D_MODEL, KW]), ("ck", [D_MODEL, KW]), ("cv", [D_MODEL, KW]),
                    ("Wf1", [D_MODEL, DV]), ("Wf2", [DV, D_MODEL]),
                    ("Wb", [D_MODEL, H]), ("Wg1", [D_MODEL, DV]), ("Wg2", [DV, D_MODEL]),
                    ("bg2", [1, D_MODEL]), ("norm_w", [128, 1]), ("Wo", [D_MODEL, D_MODEL]),
                    ("gate_w", [MOE_E, D_MODEL]),
                    ("w_gate", [MOE_E, D_MODEL, 1408]), ("w_up", [MOE_E, D_MODEL, 1408]),
                    ("w_down", [MOE_E, 1408, D_MODEL]),
                    ("sw_gate", [D_MODEL, 2816]), ("sw_up", [D_MODEL, 2816]),
                    ("sw_down", [2816, D_MODEL])]:
        wd[nm] = nc.dram_tensor(nm, shp, F32, kind="ExternalInput").ap()
    dIDN = nc.dram_tensor("cIDN", [128, 128], BF16, kind="ExternalInput").ap()
    dLT2P = nc.dram_tensor("cLT2P", [128, 128], BF16, kind="ExternalInput").ap()
    dMIDS = nc.dram_tensor("cMIDS", [128, 4], BF16, kind="ExternalInput").ap()
    dMASKP = nc.dram_tensor("cMASKP", [128, 128], BF16, kind="ExternalInput").ap()
    dONES1 = nc.dram_tensor("cONES1", [1, 128], BF16, kind="ExternalInput").ap()
    dSMASK = nc.dram_tensor("cSMASK", [128, 8], F32, kind="ExternalInput").ap()
    dCF = nc.dram_tensor("cF32", [128, 8], F32, kind="ExternalInput").ap()
    dout = nc.dram_tensor("out", [T_CORE, D_MODEL], F32, kind="ExternalOutput").ap()
    dspill = nc.dram_tensor("xspill", [T_CORE, D_MODEL], F32, kind="Internal").ap()
    dcc_in = [nc.dram_tensor(f"cc_in{g}", [HPG * 128, 257], F32, kind="Internal").ap()
              for g in range(NHG)]
    dcc_out = [nc.dram_tensor(f"cc_out{g}", [4 * HPG * 128, 257], F32,
                              kind="Internal", addr_space="Local").ap()
               for g in range(NHG)]

    prog = Prog(nc)
    P = prog
    tap_outs = {}

    def mktap(name, src_ap, shape, dtype, reads):
        t = nc.dram_tensor("tap_" + name, shape, dtype, kind="ExternalOutput").ap()
        tap_outs[name] = t
        tag = "tp" + name
        P.dma("sync", t, src_ap, tag, reads=reads)
        P.final_tags.append(tag)

    import contextlib
    stack = contextlib.ExitStack()
    SB = lambda nm, shp, dt: stack.enter_context(nc.sbuf_tensor(nm, shp, dt))
    PS = lambda nm, shp, dt: stack.enter_context(nc.psum_tensor(nm, shp, dt))

    with stack:
        xbuf = SB("xbuf", [128, NTG * D_MODEL], F32)
        combine = SB("combine", [128, NTG * MOE_E], F32)
        statv = SB("statv", [128, 8], F32)
        vws = SB("vws", [128, 64], F32)
        fws = SB("fws", [128, 64], F32)
        hbuf = SB("hbuf", [128, NCH * 1027], BF16)
        sar = SB("sharena", [128, 53312], BF16)
        g1b = SB("g1b", [128, T_CORE], BF16)
        gt1b = SB("gt1b", [128, T_CORE], BF16)
        wbt = SB("wbt", [128, NCH * H], BF16)
        gwt = SB("gwt", [128, NCH * MOE_E], BF16)
        nwb = SB("nwb", [128, 1], F32)
        cIDN = SB("kIDN", [128, 128], BF16)
        cLT2P = SB("kLT2P", [128, 128], BF16)
        cMIDS = SB("kMIDS", [128, 4], BF16)
        cMASKP = SB("kMASKP", [128, 128], BF16)
        cONES1 = SB("kONES1", [1, 128], BF16)
        cSMASK = SB("kSMASK", [128, 8], F32)
        cf = SB("kCF", [128, 8], F32)
        dbgs = SB("dbgs", [128, 80], F32)

        OF_Q, OF_K, OF_UTM, OF_KTM, OF_VTM = 0, 4096, 8192, 12288, 16384
        OF_GTM = 20480
        OF_WS1, OF_WS2 = 24576, 26624
        OF_WSL = 28672
        OF_HQ = 32768
        OF_WS0 = 34880
        OF_OFM = 36928
        EO_WO = 0
        FF_A = 0
        FF_WD = 11264
        FF_GWS = 11264
        FF_WGU = 33792
        FF_WS = 41984

        XB_AGR, XB_STZ, XB_EV = 0, 4112, 5140
        XB_CONV = 6420
        XB_XH, XB_BETA, XB_CC = 8980, 11028, 11156
        XB_WSLB, XB_SG = 11348, 13396
        agrecv = xbuf[:, XB_AGR:XB_AGR + 4112]
        stz = xbuf[:, XB_STZ:XB_STZ + 1028]
        xh = xbuf[0:3, XB_XH:XB_XH + 2048]
        betab = xbuf[:, XB_BETA:XB_BETA + 128]
        ccb = {k: xbuf[:, XB_CC + 64 * i: XB_CC + 64 * (i + 1)]
               for i, k in enumerate(("cq", "ck", "cv"))}
        wslB = xbuf[:, XB_WSLB:XB_WSLB + 2048].bitcast(BF16)
        sgv = xbuf[:, XB_SG:XB_SG + 2048].bitcast(BF16)
        XRES = (["agrecv", "evec", "conv0", "conv1", "xh",
                 "betab", "cccq", "ccck", "cccv", "wslB", "sgate"]
                + [f"st{h}" for h in range(HPG)]
                + [f"ev{h}" for h in range(HPG)]
                + [f"xw{h}" for h in range(HPG)])

        wf1t = sar[:, OF_WSL:OF_WSL + 2048]
        wg1t = sar[:, OF_WSL + 2048:OF_WSL + 4096]
        wf2sl = sar[:, OF_WS0:OF_WS0 + 512]
        wg2sl = sar[:, OF_WS0 + 512:OF_WS0 + 1024]
        bg2sl = sar[0:1, OF_WS0 + 1024:OF_WS0 + 1536]
        gwsb = sar[0:MOE_E, FF_GWS:FF_GWS + 2048]
        WSBASE = {0: OF_WS0, 1: OF_WS1, 2: OF_WS2, 3: OF_HQ}

        def wsb(hh, i):
            return sar[:, WSBASE[hh] + i * 128: WSBASE[hh] + (i + 1) * 128]

        def xws(hh, j):
            return xbuf[:, XB_CONV + (hh * 4 + j) * 128:
                        XB_CONV + (hh * 4 + j + 1) * 128]

        WN = {hh: f"w{hh}" for hh in range(HPG)}

        pb = [PS(f"pb{i}", [128, 512], F32) for i in range(4)]
        pt = [PS(f"pt{i}", [128, 1024], BF16) for i in range(4)]
        ptf = [t[:].bitcast(F32) for t in pt]
        PBN = [f"pb{i}" for i in range(4)]
        PTN = [f"pt{i}" for i in range(4)]

        P.dma("sync", cIDN[:], dIDN, "cIDN", writes=["cIDN"])
        P.dma("sync", cLT2P[:], dLT2P, "cLT2P", writes=["cLT2P"])
        P.dma("sync", cMIDS[:], dMIDS, "cMIDS", writes=["cMIDS"])
        P.dma("sync", cMASKP[:], dMASKP, "cMASKP", writes=["cMASKP"])
        P.dma("sync", cONES1[:], dONES1, "cONES1", writes=["cONES1"])
        P.dma("sync", cSMASK[:], dSMASK, "cSMASK", writes=["cSMASK"])
        P.dma("sync", cf[:], dCF, "cf", writes=["cf"])
        P.dma("sync", nwb[:], wd["norm_w"], "nwb", writes=["nwb"])
        P.dma("gpsimd", wf1t.rearrange("p (c v) -> p c v", c=NCH),
              wd["Wf1"].rearrange("(c p) v -> p c v", p=128), "wf1", writes=["wf1t"])
        P.dma("gpsimd", wg1t.rearrange("p (c v) -> p c v", c=NCH),
              wd["Wg1"].rearrange("(c p) v -> p c v", p=128), "wg1", writes=["wg1t"])
        P.dma("gpsimd", wbt[:].rearrange("p (c e) -> p c e", c=NCH),
              wd["Wb"].rearrange("(c p) e -> p c e", p=128), "wb", writes=["wbt"])
        P.dma("sync", xbuf[:].rearrange("p (g d) -> p g d", g=NTG),
              dx.rearrange("(g p) d -> p g d", p=128), "x",
              writes=[f"xtg{g}" for g in range(NTG)])

        lncnt = [0]

        def layernorm_tm(src_res, src_ap, dst_ap, dst_res, rows=128):
            s_ = statv
            junk_ap = sar[0:rows, OF_OFM:OF_OFM + D_MODEL]
            P.red(s_[0:rows, 0:1], src_ap, OP.add, reads=[src_res], writes=["stat"])
            P.act(junk_ap, src_ap, AF.Square, accum=s_[0:rows, 1:2],
                  reads=[src_res], writes=["stat", "junkws"])
            P.ts("vector", s_[0:rows, 2:3], s_[0:rows, 0:1], 1.0 / D_MODEL, OP.mult,
                 reads=["stat"], writes=["stat"])
            P.ts("vector", s_[0:rows, 4:5], s_[0:rows, 1:2], 1.0 / D_MODEL, OP.mult,
                 reads=["stat"], writes=["stat"])
            P.tt("vector", s_[0:rows, 3:4], s_[0:rows, 2:3], s_[0:rows, 2:3], OP.mult,
                 reads=["stat"], writes=["stat"])
            P.tt("vector", s_[0:rows, 4:5], s_[0:rows, 4:5], s_[0:rows, 3:4],
                 OP.subtract, reads=["stat"], writes=["stat"])
            P.act(s_[0:rows, 5:6], s_[0:rows, 4:5], AF.Sqrt, bias=cf[0:rows, 0:1],
                  reads=["stat", "cf"], writes=["stat"])
            P.recip(s_[0:rows, 5:6], reads=["stat"], writes=["stat"])
            P.ts("vector", s_[0:rows, 6:7], s_[0:rows, 2:3], s_[0:rows, 5:6], OP.mult,
                 reads=["stat"], writes=["stat"])
            P.ts("vector", s_[0:rows, 7:8], s_[0:rows, 6:7], -1.0, OP.mult,
                 reads=["stat"], writes=["stat"])
            P.ts("vector", dst_ap, src_ap, s_[0:rows, 5:6], OP.mult,
                 s_[0:rows, 7:8], OP.add, reads=[src_res, "stat"], writes=[dst_res])
            if "dbgs" in taps and lncnt[0] < 10:
                i = lncnt[0]
                P.cp("vector", dbgs[0:rows, i * 8:(i + 1) * 8], s_[0:rows, 0:8],
                     reads=["stat"], writes=["dbgs"])
            lncnt[0] += 1

        def htg_ap(i):
            return sar[:, OF_GTM + i * D_MODEL: OF_GTM + (i + 1) * D_MODEL]

        for tg in range(NTG):
            sl = htg_ap(tg % 2)
            layernorm_tm(f"xtg{tg}", xbuf[:, tg * D_MODEL:(tg + 1) * D_MODEL], sl,
                         f"htg{tg % 2}")
            for c in range(NCH):
                ptile = pt[c % 2]
                P.transp(ptile[:, 0:128], sl[:, c * 128:(c + 1) * 128], cIDN[:],
                         reads=[f"htg{tg % 2}", "cIDN"], writes=[PTN[c % 2]])
                P.cp("vector",
                     hbuf[:, c * 1027 + 3 + tg * 128: c * 1027 + 3 + (tg + 1) * 128],
                     ptile[:, 0:128], reads=[PTN[c % 2]], writes=["hbuf"])
        spill_ev = P.dma("sync", dspill.rearrange("(g p) d -> p g d", p=128),
                         xbuf[:].rearrange("p (g d) -> p g d", g=NTG), "spill",
                         reads=[f"xtg{g}" for g in range(NTG)])
        P.inherit(XRES, [f"xtg{g}" for g in range(NTG)])
        P.dma("sync", xh, dhalo, "xh", writes=["xh"])
        layernorm_tm("xh", xh, htg_ap(0)[0:3, 0:D_MODEL], "htg0", rows=3)
        for c in range(NCH):
            ptile = pt[c % 2]
            P.transp(ptile[0:128, 0:3], htg_ap(0)[0:3, c * 128:(c + 1) * 128],
                     cIDN[0:3, 0:3], reads=["htg0", "cIDN"], writes=[PTN[c % 2]])
            P.cp("vector", hbuf[:, c * 1027: c * 1027 + 3], ptile[:, 0:3],
                 reads=[PTN[c % 2]], writes=["hbuf"])
        for k in ("cq", "ck", "cv"):
            P.dma("sync", ccb[k].rearrange("p (c j) -> p c j", c=NCH),
                  wd[k].rearrange("(c p) j -> p c j", p=128), "cc" + k,
                  writes=["cc" + k])

        if "dbgs" in taps:
            mktap("dbgs", dbgs[:], [128, 80], F32, ["dbgs"])
        if "hbuf" in taps:
            mktap("hbuf", hbuf[:], [128, NCH * 1027], BF16, ["hbuf"])

        def proj_fm(dst_ap, dst_res, wtiles, wres, psum_i):
            for w_i, base in enumerate((3, 515)):
                ps = pb[(psum_i + w_i) % 4]
                psn = PBN[(psum_i + w_i) % 4]
                for kc in range(NCH):
                    P.mm(ps[:, 0:512], wtiles[:, kc * 128:(kc + 1) * 128],
                         hbuf[:, kc * 1027 + base: kc * 1027 + base + 512],
                         start=(kc == 0), stop=(kc == NCH - 1),
                         reads=[wres, "hbuf"], writes=[psn])
                P.cp("vector", dst_ap[:, w_i * 512:(w_i + 1) * 512], ps[:, 0:512],
                     reads=[psn], writes=[dst_res])
        proj_fm(g1b[:], "g1b", wf1t, "wf1t", 0)
        proj_fm(gt1b[:], "gt1b", wg1t, "wg1t", 2)
        for tg in range(NTG):
            ps = ptf[tg % 2]
            psn = PTN[tg % 2]
            for kc in range(NCH):
                P.mm(ps[:, 0:H],
                     hbuf[:, kc * 1027 + 3 + tg * 128: kc * 1027 + 3 + (tg + 1) * 128],
                     wbt[:, kc * H:(kc + 1) * H],
                     start=(kc == 0), stop=(kc == NCH - 1),
                     reads=["hbuf", "wbt"], writes=[psn])
            P.act(betab[:, tg * H:(tg + 1) * H], ps[:, 0:H], AF.Sigmoid,
                  reads=[psn], writes=["betab"])
        if "beta" in taps:
            mktap("beta", betab, [128, NTG * H], F32, ["betab"])

        P.inherit(["gtm"], ["htg0", "htg1"])
        P.inherit(["wslA"], ["wf1t", "wg1t"])
        P.inherit(["ofm"], ["junkws"])

        def phase_A(hg):
            P.inherit(["wf2sl", "wg2sl", "bg2sl"], [WN[0]])
            P.inherit(["hq0", "hq1"], [WN[3]])
            P.inherit(["conv0", "conv1"], [f"xw{h}" for h in range(HPG)])
            P.dma("gpsimd", wf2sl, wd["Wf2"][:, hg * 512:(hg + 1) * 512], "wf2",
                  writes=["wf2sl"])
            P.dma("gpsimd", wg2sl, wd["Wg2"][:, hg * 512:(hg + 1) * 512], "wg2",
                  writes=["wg2sl"])
            P.dma("gpsimd", bg2sl, wd["bg2"][:, hg * 512:(hg + 1) * 512], "bg2",
                  writes=["bg2sl"])
            for si, (wn, ccn) in enumerate((("Wq", "cq"), ("Wk", "ck"),
                                            ("Wv", "cv"), ("Ww", "cv"))):
                ofs = (OF_Q, OF_K, OF_UTM, OF_KTM)[si]
                sres = ("q4", "k4", "vr4", "wr4")[si]
                P.dma("gpsimd",
                      sar[:, OF_WSL: OF_WSL + 4096].rearrange("p (c n) -> p c n", c=8),
                      wd[wn][0:1024, hg * 512:(hg + 1) * 512]
                      .rearrange("(c p) n -> p c n", p=128),
                      "wslA", writes=["wslA"])
                P.dma("gpsimd", wslB.rearrange("p (c n) -> p c n", c=8),
                      wd[wn][1024:2048, hg * 512:(hg + 1) * 512]
                      .rearrange("(c p) n -> p c n", p=128),
                      "wslB", writes=["wslB"])
                for hh in range(HPG):
                    h_abs = hg * HPG + hh
                    hq0 = OF_HQ + (hh % 2) * 1027
                    hqres = f"hq{hh % 2}"
                    for w_i, (base, n) in enumerate(((0, 512), (512, 512), (1024, 3))):
                        ps = pb[w_i % 3]
                        psn = PBN[w_i % 3]
                        for kc in range(NCH):
                            if kc < 8:
                                wtile = sar[:, OF_WSL + kc * 512 + hh * 128:
                                            OF_WSL + kc * 512 + (hh + 1) * 128]
                                wres = "wslA"
                            else:
                                wtile = wslB[:, (kc - 8) * 512 + hh * 128:
                                             (kc - 8) * 512 + (hh + 1) * 128]
                                wres = "wslB"
                            P.mm(ps[:, 0:n], wtile,
                                 hbuf[:, kc * 1027 + base: kc * 1027 + base + n],
                                 start=(kc == 0), stop=(kc == NCH - 1),
                                 reads=[wres, "hbuf"], writes=[psn])
                        P.act(sar[:, hq0 + base: hq0 + base + n], ps[:, 0:n],
                              AF.Copy, reads=[psn], writes=[hqres])
                    cc = ccb[ccn]
                    cacc = xbuf[:, XB_CONV + (hh % 2) * 1024:
                                XB_CONV + (hh % 2 + 1) * 1024]
                    caccres = f"conv{hh % 2}"
                    hq = sar[:, hq0: hq0 + 1027]
                    P.ts("vector", cacc, hq[:, 3:1027],
                         cc[:, h_abs * KW + 3: h_abs * KW + 4], OP.mult,
                         reads=[hqres, "cc" + ccn], writes=[caccres])
                    for j in (1, 2, 3):
                        P.stt("vector", cacc, hq[:, 3 - j: 3 - j + 1024],
                              cc[:, h_abs * KW + 3 - j: h_abs * KW + 4 - j], cacc,
                              OP.mult, OP.add, reads=[hqres, "cc" + ccn, caccres],
                              writes=[caccres])
                    dst = sar[:, ofs + hh * 1024: ofs + (hh + 1) * 1024]
                    P.act(dst, cacc, AF.Silu, reads=[caccres], writes=[sres])
            for tg in range(NTG):
                ps = pb[3] if tg % 2 == 0 else ptf[0]
                psn = PBN[3] if tg % 2 == 0 else PTN[0]
                P.mm(ps[:, 0:512], g1b[:, tg * 128:(tg + 1) * 128], wf2sl,
                     start=True, stop=True, reads=["g1b", "wf2sl"], writes=[psn])
                P.act(sar[:, OF_GTM + tg * 512: OF_GTM + (tg + 1) * 512], ps[:, 0:512],
                      AF.Sigmoid, reads=[psn], writes=["gtm"])
            for tg in range(NTG):
                gsl = sar[:, OF_GTM + tg * 512: OF_GTM + (tg + 1) * 512]
                P.act(gsl, gsl, AF.Ln, reads=["gtm"], writes=["gtm"])
            for tg in range(NTG):
                ps = pb[3] if tg % 2 == 0 else ptf[0]
                psn = PBN[3] if tg % 2 == 0 else PTN[0]
                P.mm(ps[:, 0:512], gt1b[:, tg * 128:(tg + 1) * 128], wg2sl,
                     start=True, stop=False, reads=["gt1b", "wg2sl"], writes=[psn])
                P.mm(ps[:, 0:512], cONES1[:, 0:128], bg2sl,
                     start=False, stop=True, reads=["cONES1", "bg2sl"], writes=[psn])
                P.act(sgv[:, tg * 512:(tg + 1) * 512], ps[:, 0:512],
                      AF.Sigmoid, reads=[psn], writes=["sgate"])

        def bchain(hg, hh):
            bank, bkn = pb[hh], PBN[hh]
            ptb, ptn = pt[hh], PTN[hh]
            wn = WN[hh]
            vwn, evn, stn, xwn = f"vw{hh}", f"ev{hh}", f"st{hh}", f"xw{hh}"
            vw = lambda i: vws[:, hh * 16 + i: hh * 16 + i + 1]
            skz = stz[:, hh * 257: hh * 257 + 128]
            svz = stz[:, hh * 257 + 128: hh * 257 + 256]
            gslot = stz[:, hh * 257 + 256: hh * 257 + 257]
            for p in range(NPAIR):
                h_abs = hg * HPG + hh
                gsl = sar[:, OF_GTM + p * 512 + hh * 128:
                          OF_GTM + p * 512 + (hh + 1) * 128]
                evs = xbuf[:, XB_EV + (hh * NPAIR + p) * 8:
                           XB_EV + (hh * NPAIR + p) * 8 + 8]
                utm = sar[:, OF_UTM + hh * 1024 + p * 128:
                          OF_UTM + hh * 1024 + (p + 1) * 128]
                ktm = sar[:, OF_KTM + hh * 1024 + p * 128:
                          OF_KTM + hh * 1024 + (p + 1) * 128]
                vtm = sar[:, OF_VTM + hh * 1024 + p * 128:
                          OF_VTM + hh * 1024 + (p + 1) * 128]
                ksl = sar[:, OF_K + hh * 1024 + p * 128:
                          OF_K + hh * 1024 + (p + 1) * 128]
                vraw = utm
                wraw = ktm
                bcol = betab[:, p * H + h_abs: p * H + h_abs + 1]
                P.mm(bank[:, 0:128], cLT2P[:], gsl, start=True, stop=True,
                     reads=["cLT2P", "gtm"], writes=[bkn]); yield
                P.mm(bank[:, 128:132], gsl, cMIDS[:], start=True, stop=True,
                     reads=["gtm", "cMIDS"], writes=[bkn]); yield
                P.act(wsb(hh, 0), bank[:, 0:128], AF.Exp, scale=cf[:, 3:4],
                      reads=[bkn, "cf"], writes=[wn]); yield
                P.act(evs[:, 0:4], bank[:, 128:132], AF.Exp, reads=[bkn],
                      writes=[evn]); yield
                P.tt("vector", evs[:, 4:5], evs[:, 0:1], evs[:, 1:2], OP.mult,
                     reads=[evn], writes=[evn]); yield
                P.tt("vector", evs[:, 5:6], evs[:, 2:3], evs[:, 3:4], OP.mult,
                     reads=[evn], writes=[evn]); yield
                P.tt("vector", evs[:, 6:7], evs[:, 5:6], evs[:, 1:2], OP.mult,
                     reads=[evn], writes=[evn]); yield
                P.transp(ptb[:, 0:128], vraw, cIDN[:], reads=["vr4", "cIDN"],
                         writes=[ptn]); yield
                P.cp("scalar", vtm, ptb[:, 0:128], reads=[ptn], writes=["vtm4"]); yield
                P.transp(ptb[:, 128:256], wraw, cIDN[:], reads=["wr4", "cIDN"],
                         writes=[ptn]); yield
                P.cp("scalar", wsb(hh, 1), ptb[:, 128:256], reads=[ptn],
                     writes=[wn]); yield
                P.act(wsb(hh, 2), ptb[:, 128:256], AF.Square, accum=vw(0),
                      reads=[ptn], writes=[wn, vwn]); yield
                P.act(vw(1), vw(0), AF.Sqrt, bias=cf[:, 1:2],
                      reads=[vwn, "cf"], writes=[vwn]); yield
                P.recip(vw(1), reads=[vwn], writes=[vwn]); yield
                P.stt("vector", utm, wsb(hh, 1), vw(1), wsb(hh, 0), OP.mult, OP.mult,
                      reads=[wn, vwn, "vr4"], writes=["utm4", "vr4"]); yield
                P.transp(ptb[:, 256:384], ksl, cIDN[:], reads=["k4", "cIDN"],
                         writes=[ptn]); yield
                P.cp("scalar", ktm, ptb[:, 256:384], reads=[ptn, "wr4"],
                     writes=["ktm4", "wr4"]); yield
                P.ts("vector", wsb(hh, 3), utm, bcol, OP.mult, cf[:, 5:6], OP.mult,
                     reads=["utm4", "betab", "cf"], writes=[wn]); yield
                P.ts("vector", wsb(hh, 4), utm, bcol, OP.mult, cf[:, 6:7], OP.mult,
                     reads=["utm4", "betab", "cf"], writes=[wn]); yield
                ubA, ubB = wsb(hh, 3), wsb(hh, 4)
                P.mm(bank[:, 0:128], ubA, ktm, start=True,
                     stop=True, reads=[wn, "ktm4"], writes=[bkn]); yield
                P.mm(bank[:, 128:256], ubA, vtm, start=True,
                     stop=True, reads=[wn, "vtm4"], writes=[bkn]); yield
                P.mm(bank[:, 256:384], ubB, ktm, start=True,
                     stop=True, reads=[wn, "ktm4"], writes=[bkn]); yield
                P.mm(bank[:, 384:512], ubB, vtm, start=True,
                     stop=True, reads=[wn, "vtm4"], writes=[bkn]); yield
                P.tt("vector", vw(2), evs[:, 4:5], evs[:, 5:6], OP.mult,
                     reads=[evn], writes=[vwn]); yield
                for st_i, ssl in ((0, skz), (1, svz)):
                    tws = xws(hh, 3)
                    P.ts("vector", tws, bank[:, st_i * 128:(st_i + 1) * 128],
                         evs[:, 6:7], OP.mult, reads=[bkn, evn],
                         writes=[xwn]); yield
                    P.stt("vector", tws, bank[:, 256 + st_i * 128: 384 + st_i * 128],
                          evs[:, 3:4], tws, OP.mult, OP.add,
                          reads=[bkn, evn, xwn], writes=[xwn]); yield
                    P.stt("vector", ssl, ssl, vw(2), tws, OP.mult, OP.add,
                          reads=[vwn, xwn], writes=[stn]); yield
                if p == 0:
                    P.ts("vector", gslot, vw(2), 1.0, OP.mult,
                         reads=[vwn], writes=[stn]); yield
                else:
                    P.tt("vector", gslot, gslot, vw(2), OP.mult,
                         reads=[vwn], writes=[stn]); yield

        def phase_AG(hg):
            P.dma("sync", dcc_in[hg].rearrange("(h p) c -> p h c", p=128),
                  stz.rearrange("p (h c) -> p h c", h=HPG), f"agi{hg}",
                  reads=[f"st{h}" for h in range(HPG)], writes=[f"ccin{hg}"])
            P.op("gpsimd",
                 lambda eng: eng.collective_compute(
                     "AllGather", OP.bypass, replica_groups=RG,
                     ins=[dcc_in[hg]], outs=[dcc_out[hg]]),
                 reads=[f"ccin{hg}"], writes=[f"ccout{hg}"])
            P.dma("sync", agrecv.rearrange("p (r c) -> p r c", r=4 * HPG),
                  dcc_out[hg].rearrange("(r p) c -> p r c", p=128), f"ago{hg}",
                  reads=[f"ccout{hg}"], writes=["agrecv"])
            for r in range(4):
                for hh in range(HPG):
                    base = (r * HPG + hh) * 257
                    gr = agrecv[:, base + 256: base + 257]
                    mcol = cSMASK[:, r: r + 1]
                    omcol = cSMASK[:, 4 + r: 5 + r]
                    gh = vws[:, hh * 16 + 10: hh * 16 + 11]
                    vwn, stn, xwn = f"vw{hh}", f"st{hh}", f"xw{hh}"
                    if r > 0:
                        P.ts("vector", gh, gr, mcol, OP.mult,
                             reads=["agrecv", "cSMASK"], writes=[vwn])
                        P.tt("vector", gh, gh, omcol, OP.add,
                             reads=["cSMASK"], writes=[vwn])
                    for st_i in range(2):
                        ssl = stz[:, hh * 257 + st_i * 128:
                                  hh * 257 + (st_i + 1) * 128]
                        srr = agrecv[:, base + st_i * 128: base + (st_i + 1) * 128]
                        u1 = xws(hh, 3)
                        P.ts("vector", u1, srr, mcol, OP.mult,
                             reads=["agrecv", "cSMASK"], writes=[xwn])
                        if r == 0:
                            P.cp("vector", ssl, u1, reads=[xwn], writes=[stn])
                        else:
                            P.stt("vector", ssl, ssl, gh, u1, OP.mult, OP.add,
                                  reads=[vwn, xwn], writes=[stn])

        def dchain(hg, hh):
            bank, bkn = pb[hh], PBN[hh]
            ptb, ptn = pt[hh], PTN[hh]
            wn = WN[hh]
            vwn, evn, stn, xwn = f"vw{hh}", f"ev{hh}", f"st{hh}", f"xw{hh}"
            vw = lambda i: vws[:, hh * 16 + i: hh * 16 + i + 1]
            skz = stz[:, hh * 257: hh * 257 + 128]
            svz = stz[:, hh * 257 + 128: hh * 257 + 256]
            for p in range(NPAIR):
                h_abs = hg * HPG + hh
                gsl = sar[:, OF_GTM + p * 512 + hh * 128:
                          OF_GTM + p * 512 + (hh + 1) * 128]
                evs = xbuf[:, XB_EV + (hh * NPAIR + p) * 8:
                           XB_EV + (hh * NPAIR + p) * 8 + 8]
                utm = sar[:, OF_UTM + hh * 1024 + p * 128:
                          OF_UTM + hh * 1024 + (p + 1) * 128]
                ktm = sar[:, OF_KTM + hh * 1024 + p * 128:
                          OF_KTM + hh * 1024 + (p + 1) * 128]
                vtm = sar[:, OF_VTM + hh * 1024 + p * 128:
                          OF_VTM + hh * 1024 + (p + 1) * 128]
                qsl = sar[:, OF_Q + hh * 1024 + p * 128:
                          OF_Q + hh * 1024 + (p + 1) * 128]
                ksl = sar[:, OF_K + hh * 1024 + p * 128:
                          OF_K + hh * 1024 + (p + 1) * 128]
                bcol = betab[:, p * H + h_abs: p * H + h_abs + 1]
                P.ts("vector", wsb(hh, 0), skz, evs[:, 0:1], OP.mult,
                     reads=[stn, evn], writes=[wn]); yield
                P.ts("vector", wsb(hh, 1), svz, evs[:, 0:1], OP.mult,
                     reads=[stn, evn], writes=[wn]); yield
                P.ts("vector", wsb(hh, 2), utm, bcol, OP.mult, cf[:, 5:6], OP.mult,
                     reads=["utm4", "betab", "cf"], writes=[wn]); yield
                P.ts("vector", wsb(hh, 15), utm, bcol, OP.mult, cf[:, 6:7], OP.mult,
                     reads=["utm4", "betab", "cf"], writes=[wn]); yield
                ubA, ubB = wsb(hh, 2), wsb(hh, 15)
                P.mm(bank[:, 0:128], ubA, ktm, start=True,
                     stop=True, reads=[wn, "ktm4"], writes=[bkn]); yield
                P.mm(bank[:, 128:256], ubA, vtm, start=True,
                     stop=True, reads=[wn, "vtm4"], writes=[bkn]); yield
                for st_i, ssl in ((0, skz), (1, svz)):
                    tws = xws(hh, 3)
                    P.ts("vector", tws, bank[:, st_i * 128:(st_i + 1) * 128],
                         evs[:, 1:2], OP.mult, reads=[bkn, evn],
                         writes=[xwn]); yield
                    P.stt("vector", ssl, ssl, evs[:, 4:5], tws, OP.mult, OP.add,
                          reads=[evn, xwn], writes=[stn]); yield
                P.ts("vector", wsb(hh, 3), skz, evs[:, 2:3], OP.mult,
                     reads=[stn, evn], writes=[wn]); yield
                P.ts("vector", wsb(hh, 4), svz, evs[:, 2:3], OP.mult,
                     reads=[stn, evn], writes=[wn]); yield
                P.mm(bank[:, 256:384], ubB, ktm, start=True,
                     stop=True, reads=[wn, "ktm4"], writes=[bkn]); yield
                P.mm(bank[:, 384:512], ubB, vtm, start=True,
                     stop=True, reads=[wn, "vtm4"], writes=[bkn]); yield
                for st_i, ssl in ((0, skz), (1, svz)):
                    tws = xws(hh, 3)
                    P.ts("vector", tws, bank[:, 256 + st_i * 128: 384 + st_i * 128],
                         evs[:, 3:4], OP.mult, reads=[bkn, evn],
                         writes=[xwn]); yield
                    P.stt("vector", ssl, ssl, evs[:, 5:6], tws, OP.mult, OP.add,
                          reads=[evn, xwn], writes=[stn]); yield
                P.transp(ptb[:, 0:128], wsb(hh, 0), cIDN[:], reads=[wn, "cIDN"],
                         writes=[ptn]); yield
                P.cp("scalar", wsb(hh, 5), ptb[:, 0:128], reads=[ptn],
                     writes=[wn]); yield
                P.transp(ptb[:, 128:256], wsb(hh, 3), cIDN[:], reads=[wn, "cIDN"],
                         writes=[ptn]); yield
                P.cp("scalar", wsb(hh, 6), ptb[:, 128:256], reads=[ptn],
                     writes=[wn]); yield
                P.transp(ptb[:, 256:384], utm, cIDN[:], reads=["utm4", "cIDN"],
                         writes=[ptn]); yield
                P.cp("scalar", wsb(hh, 7), ptb[:, 256:384], reads=[ptn],
                     writes=[wn]); yield
                P.mm(bank[:, 0:128], cLT2P[:], gsl, start=True, stop=True,
                     reads=["cLT2P", "gtm"], writes=[bkn]); yield
                P.act(wsb(hh, 8), bank[:, 0:128], AF.Exp, bias=cf[:, 2:3],
                      reads=[bkn, "cf"], writes=[wn]); yield
                P.mm(bank[:, 128:256], ksl, qsl, start=True, stop=True,
                     reads=["k4", "q4"], writes=[bkn]); yield
                P.ts("vector", wsb(hh, 9), cMASKP[:], bcol, OP.mult,
                     reads=["cMASKP", "betab"], writes=[wn]); yield
                P.tt("vector", wsb(hh, 10), bank[:, 128:256], wsb(hh, 9), OP.mult,
                     reads=[bkn, wn], writes=[wn]); yield
                P.mm(bank[:, 256:384], wsb(hh, 10), utm, start=True, stop=False,
                     reads=[wn, "utm4"], writes=[bkn]); yield
                P.mm(bank[0:64, 256:384], qsl[:, 0:64], wsb(hh, 5), start=False,
                     stop=False, reads=["q4", wn], writes=[bkn]); yield
                P.mm(bank[64:128, 256:384], qsl[:, 64:128], wsb(hh, 6), start=False,
                     stop=True, reads=["q4", wn], writes=[bkn]); yield
                s_sb = xws(hh, 1)
                P.tt("vector", s_sb, bank[:, 256:384], wsb(hh, 8), OP.mult,
                     reads=[bkn, wn], writes=[xwn]); yield
                P.red(vw(4), s_sb, OP.max, negate=True,
                      reads=[xwn], writes=[vwn]); yield
                P.act(wsb(hh, 11), s_sb, AF.Exp, bias=vw(4),
                      reads=[xwn, vwn], writes=[wn]); yield
                P.red(vw(5), wsb(hh, 11), OP.add, reads=[wn], writes=[vwn]); yield
                P.ts("vector", vw(6), vw(5), SCALE, OP.mult,
                     reads=[vwn], writes=[vwn]); yield
                P.recip(vw(6), reads=[vwn], writes=[vwn]); yield
                P.tt("vector", wsb(hh, 12), wsb(hh, 11), wsb(hh, 8), OP.mult,
                     reads=[wn], writes=[wn]); yield
                P.transp(ptb[:, 384:512], wsb(hh, 12), cIDN[:], reads=[wn, "cIDN"],
                         writes=[ptn]); yield
                P.cp("scalar", wsb(hh, 13), ptb[:, 384:512], reads=[ptn],
                     writes=[wn]); yield
                P.mm(bank[:, 384:512], wsb(hh, 7), wsb(hh, 13), start=True,
                     stop=True, reads=[wn], writes=[bkn]); yield
                P.tt("vector", wsb(hh, 14), bank[:, 384:512], wsb(hh, 9), OP.mult,
                     reads=[bkn, wn], writes=[wn]); yield
                P.mm(bank[:, 0:128], wsb(hh, 14), vtm, start=True, stop=False,
                     reads=[wn, "vtm4"], writes=[bkn]); yield
                P.mm(bank[0:64, 0:128], wsb(hh, 13)[:, 0:64], wsb(hh, 1),
                     start=False, stop=False, reads=[wn], writes=[bkn]); yield
                P.mm(bank[64:128, 0:128], wsb(hh, 13)[:, 64:128], wsb(hh, 4),
                     start=False, stop=True, reads=[wn], writes=[bkn]); yield
                o1 = xws(hh, 2)
                P.ts("vector", o1, bank[:, 0:128], vw(6), OP.mult,
                     reads=[bkn, vwn], writes=[xwn]); yield
                P.act(wsb(hh, 15), o1, AF.Square, accum=vw(7),
                      reads=[xwn], writes=[wn, vwn]); yield
                P.act(vw(8), vw(7), AF.Sqrt, bias=cf[:, 0:1], scale=cf[:, 4:5],
                      reads=[vwn, "cf"], writes=[vwn]); yield
                P.recip(vw(8), reads=[vwn], writes=[vwn]); yield
                sg = sgv[:, p * 512 + hh * 128: p * 512 + (hh + 1) * 128]
                P.stt("vector", wsb(hh, 12), o1, vw(8), sg, OP.mult, OP.mult,
                      reads=[xwn, vwn, "sgate"], writes=[wn]); yield
                P.transp(ptb[:, 0:128], wsb(hh, 12), cIDN[:], reads=[wn, "cIDN"],
                         writes=[ptn]); yield
                P.cp("vector", sar[:, OF_OFM + h_abs * 1024 + p * 128:
                                   OF_OFM + h_abs * 1024 + (p + 1) * 128],
                     ptb[:, 0:128], reads=[ptn], writes=["ofm"]); yield

        def phase_B(hg):
            P.inherit([WN[0]], ["wf2sl", "wg2sl", "bg2sl"])
            P.inherit([WN[3]], ["hq0", "hq1"])
            P.inherit([f"xw{h}" for h in range(HPG)], ["conv0", "conv1"])
            for hh in range(HPG):
                P.op("gpsimd",
                     lambda eng, a=stz[:, hh * 257:(hh + 1) * 257]: eng.memset(a, 0.0),
                     writes=[f"st{hh}"])
            drive([bchain(hg, hh) for hh in range(HPG)])

        def phase_D(hg):
            drive([dchain(hg, hh) for hh in range(HPG)])

        for hg in range(NHG):
            phase_A(hg)
            if not SKIP_SCAN:
                phase_B(hg)
                if not SKIP_AG:
                    phase_AG(hg)
                if not SKIP_D:
                    phase_D(hg)
        if SKIP_SCAN or SKIP_D:
            P.op("gpsimd", lambda eng: eng.memset(sar[:, OF_OFM:OF_OFM + 16384], 0.0),
                 reads=["junkws"], writes=["ofm"])

        if "ofm" in taps:
            mktap("ofm", sar[:, OF_OFM:OF_OFM + 16384], [128, H * T_CORE], BF16,
                  ["ofm"])
        if "q4" in taps:
            mktap("q4", sar[:, OF_Q:OF_Q + 4096], [128, HPG * T_CORE], BF16, ["q4"])
        if "gtm" in taps:
            mktap("gtm", sar[:, OF_GTM:OF_GTM + 4096], [128, NTG * 512], BF16,
                  ["gtm"])
        if "sgate" in taps:
            mktap("sgate", sgv[:, 0:4096], [128, NTG * 512], BF16, ["sgate"])
        if "utm" in taps:
            mktap("utm", sar[:, OF_UTM:OF_UTM + 4096], [128, HPG * T_CORE], BF16,
                  ["utm4"])

        P.dma("sync", xbuf[:].rearrange("p (g d) -> p g d", g=NTG),
              dspill.rearrange("(g p) d -> p g d", p=128), "unspill",
              writes=[f"xtg{g}" for g in range(NTG)] + XRES,
              extra_deps=[spill_ev])
        P.inherit(["wo0", "wo1"], ["q4", "k4", "utm4", "ktm4"])
        for dt_ in range(4):
            slot = dt_ % 2
            wosl = sar[:, EO_WO + slot * 8192: EO_WO + (slot + 1) * 8192]
            P.dma("gpsimd", wosl.rearrange("p (c n) -> p c n", c=NCH),
                  wd["Wo"][:, dt_ * 512:(dt_ + 1) * 512]
                  .rearrange("(c p) n -> p c n", p=128),
                  f"wo{slot}", writes=[f"wo{slot}"])
            P.ts("vector", wosl, wosl, nwb[:, 0:1], OP.mult,
                 reads=[f"wo{slot}", "nwb"], writes=[f"wo{slot}"])
            for tg in range(NTG):
                ps = pb[tg % 4]
                psn = PBN[tg % 4]
                for c in range(NCH):
                    P.mm(ps[:, 0:512],
                         sar[:, OF_OFM + c * 1024 + tg * 128:
                             OF_OFM + c * 1024 + (tg + 1) * 128],
                         wosl[:, c * 512:(c + 1) * 512],
                         start=(c == 0), stop=(c == NCH - 1),
                         reads=["ofm", f"wo{slot}"], writes=[psn])
                xsl = xbuf[:, tg * D_MODEL + dt_ * 512: tg * D_MODEL + (dt_ + 1) * 512]
                P.tt("vector", xsl, ps[:, 0:512], xsl, OP.add,
                     reads=[psn, f"xtg{tg}"], writes=[f"xtg{tg}"])
        if "x2" in taps:
            mktap("x2", xbuf[:], [128, NTG * D_MODEL], F32,
                  [f"xtg{g}" for g in range(NTG)])

        P.inherit(["htg0", "htg1"], ["gtm"])
        P.inherit(["junkws"], ["ofm"])
        P.inherit(["h2"], ["hbuf"])
        for tg in range(NTG):
            sl = htg_ap(tg % 2)
            layernorm_tm(f"xtg{tg}", xbuf[:, tg * D_MODEL:(tg + 1) * D_MODEL], sl,
                         f"htg{tg % 2}")
            for c in range(NCH):
                ptile = pt[c % 2]
                P.transp(ptile[:, 0:128], sl[:, c * 128:(c + 1) * 128], cIDN[:],
                         reads=[f"htg{tg % 2}", "cIDN"], writes=[PTN[c % 2]])
                P.cp("vector", hbuf[:, c * 1024 + tg * 128: c * 1024 + (tg + 1) * 128],
                     ptile[:, 0:128], reads=[PTN[c % 2]], writes=["h2"])
        if "h2" in taps:
            mktap("h2", hbuf[:, 0:16384], [128, 16384], BF16, ["h2"])
        P.inherit(["gwsb"], ["utm4", "ktm4", "wo0", "wo1"])
        P.dma("gpsimd", gwsb, wd["gate_w"], "gw", writes=["gwsb"])
        for c in range(NCH):
            ptile = pt[2 + c % 2]
            P.transp(ptile[:, 0:MOE_E], gwsb[:, c * 128:(c + 1) * 128],
                     cIDN[0:MOE_E, 0:MOE_E], reads=["gwsb", "cIDN"],
                     writes=[PTN[2 + c % 2]])
            P.cp("vector", gwt[:, c * MOE_E:(c + 1) * MOE_E], ptile[:, 0:MOE_E],
                 reads=[PTN[2 + c % 2]], writes=["gwt"])
        for tg in range(NTG):
            ps = ptf[2 + tg % 2]
            psn = PTN[2 + tg % 2]
            for kc in range(NCH):
                P.mm(ps[:, 0:MOE_E],
                     hbuf[:, kc * 1024 + tg * 128: kc * 1024 + (tg + 1) * 128],
                     gwt[:, kc * MOE_E:(kc + 1) * MOE_E],
                     start=(kc == 0), stop=(kc == NCH - 1),
                     reads=["h2", "gwt"], writes=[psn])
            P.red(fws[:, 32:33], ps[:, 0:MOE_E], OP.max, negate=True,
                  reads=[psn], writes=["fws"])
            P.act(fws[:, 0:MOE_E], ps[:, 0:MOE_E], AF.Exp, bias=fws[:, 32:33],
                  reads=[psn, "fws"], writes=["fws"])
            P.cp("vector", fws[:, 16:32], fws[:, 0:16], reads=["fws"], writes=["fws"])
            for _it in range(3):
                P.red(fws[:, 33:34], fws[:, 16:32], OP.max, reads=["fws"],
                      writes=["fws"])
                P.ts("vector", fws[:, 34:50], fws[:, 16:32], fws[:, 33:34], OP.is_ge,
                     reads=["fws"], writes=["fws"])
                P.ts("vector", fws[:, 34:50], fws[:, 34:50], -1e30, OP.mult,
                     reads=["fws"], writes=["fws"])
                P.tt("vector", fws[:, 16:32], fws[:, 34:50], fws[:, 16:32],
                     OP.add, reads=["fws"], writes=["fws"])
            P.red(fws[:, 33:34], fws[:, 16:32], OP.max, reads=["fws"], writes=["fws"])
            P.ts("vector", fws[:, 34:50], fws[:, 0:16], fws[:, 33:34], OP.is_ge,
                 reads=["fws"], writes=["fws"])
            P.tt("vector", fws[:, 34:50], fws[:, 34:50], fws[:, 0:16], OP.mult,
                 reads=["fws"], writes=["fws"])
            P.red(fws[:, 50:51], fws[:, 34:50], OP.add, reads=["fws"], writes=["fws"])
            P.recip(fws[:, 50:51], reads=["fws"], writes=["fws"])
            P.ts("vector", combine[:, tg * MOE_E:(tg + 1) * MOE_E], fws[:, 34:50],
                 fws[:, 50:51], OP.mult, reads=["fws"], writes=["combine"])
        if "combine" in taps:
            mktap("combine", combine[:], [128, NTG * MOE_E], F32, ["combine"])

        WALL = [WN[h] for h in range(HPG)]
        P.inherit([f"abuf{i}" for i in range(MOE_IC)] + ["aws0", "aws1"],
                  ["q4", "k4", "utm4", "ktm4", "vtm4", "gtm", "htg0", "htg1",
                   "wo0", "wo1", "ofm"] + WALL)
        P.inherit([f"wdb{i}" for i in range(MOE_IC)],
                  ["utm4", "ktm4", "vtm4", "gtm", "sgate", "gwsb",
                   "wslA", "hq0", "hq1",
                   "wf2sl", "wg2sl", "bg2sl", "htg0", "htg1",
                   "wo0", "wo1", "ofm", "junkws"] + WALL)
        P.inherit(["wgu0", "wgu1", "wgu2", "wgu3"], ["ofm", "junkws"] + WALL)

        experts = [("sh", 0, None), ("sh", 1, None)] + \
                  [("ex", 0, e) for e in range(MOE_E)]
        gsct = [0]

        def moe_expert(kind, half, e):
            IC = MOE_IC
            if kind == "sh":
                wg_src, wu_src, wdn_src = wd["sw_gate"], wd["sw_up"], wd["sw_down"]
                ic_off = half * MOE_IC
            else:
                wg_src, wu_src = wd["w_gate"][e], wd["w_up"][e]
                wdn_src = wd["w_down"][e]
                ic_off = 0
            for ic in range(IC):
                sA = gsct[0] % 4
                sB = (gsct[0] + 1) % 4
                gsct[0] += 2
                slA = sar[:, FF_WGU + sA * 2048: FF_WGU + (sA + 1) * 2048]
                slB = sar[:, FF_WGU + sB * 2048: FF_WGU + (sB + 1) * 2048]
                P.dma("gpsimd", slA.rearrange("p (c i) -> p c i", c=NCH),
                      wg_src[:, (ic_off + ic) * 128:(ic_off + ic + 1) * 128]
                      .rearrange("(c p) i -> p c i", p=128),
                      f"wgu{sA}", writes=[f"wgu{sA}"])
                P.dma("gpsimd", slB.rearrange("p (c i) -> p c i", c=NCH),
                      wu_src[:, (ic_off + ic) * 128:(ic_off + ic + 1) * 128]
                      .rearrange("(c p) i -> p c i", p=128),
                      f"wgu{sB}", writes=[f"wgu{sB}"])
                for kc in range(NCH):
                    for tt in range(2):
                        P.mm(ptf[tt][:, 0:512], slA[:, kc * 128:(kc + 1) * 128],
                             hbuf[:, kc * 1024 + tt * 512: kc * 1024 + (tt + 1) * 512],
                             start=(kc == 0), stop=(kc == NCH - 1),
                             reads=[f"wgu{sA}", "h2"], writes=[PTN[tt]])
                for kc in range(NCH):
                    for tt in range(2):
                        P.mm(ptf[2 + tt][:, 0:512], slB[:, kc * 128:(kc + 1) * 128],
                             hbuf[:, kc * 1024 + tt * 512: kc * 1024 + (tt + 1) * 512],
                             start=(kc == 0), stop=(kc == NCH - 1),
                             reads=[f"wgu{sB}", "h2"], writes=[PTN[2 + tt]])
                for tt in range(2):
                    asl = sar[:, FF_WS + tt * 512: FF_WS + (tt + 1) * 512]
                    P.act(asl, ptf[tt][:, 0:512], AF.Silu, reads=[PTN[tt]],
                          writes=[f"aws{tt}"])
                    P.tt("vector",
                         sar[:, FF_A + ic * 1024 + tt * 512:
                             FF_A + ic * 1024 + (tt + 1) * 512],
                         asl, ptf[2 + tt][:, 0:512], OP.mult,
                         reads=[f"aws{tt}", PTN[2 + tt]], writes=[f"abuf{ic}"])
            for ic in range(IC):
                P.dma("gpsimd", sar[:, FF_WD + ic * 2048: FF_WD + (ic + 1) * 2048],
                      wdn_src[(ic_off + ic) * 128:(ic_off + ic + 1) * 128, :],
                      f"wdb{ic}", writes=[f"wdb{ic}"])
            for tg in range(NTG):
                for ic in range(IC):
                    for dt_ in range(4):
                        P.mm(pb[dt_][:, 0:512],
                             sar[:, FF_A + ic * 1024 + tg * 128:
                                 FF_A + ic * 1024 + (tg + 1) * 128],
                             sar[:, FF_WD + ic * 2048 + dt_ * 512:
                                 FF_WD + ic * 2048 + (dt_ + 1) * 512],
                             start=(ic == 0), stop=(ic == IC - 1),
                             reads=[f"abuf{ic}", f"wdb{ic}"], writes=[PBN[dt_]])
                for dt_ in range(4):
                    xsl = xbuf[:, tg * D_MODEL + dt_ * 512:
                               tg * D_MODEL + (dt_ + 1) * 512]
                    if kind == "sh":
                        P.tt("vector", xsl, pb[dt_][:, 0:512], xsl, OP.add,
                             reads=[PBN[dt_]], writes=[f"xtg{tg}"])
                    else:
                        P.stt("vector", xsl, pb[dt_][:, 0:512],
                              combine[:, tg * MOE_E + e: tg * MOE_E + e + 1], xsl,
                              OP.mult, OP.add,
                              reads=[PBN[dt_], "combine"], writes=[f"xtg{tg}"])

        for kind, half, e in experts:
            moe_expert(kind, half, e)

        P.dma("sync", dout.rearrange("(g p) d -> p g d", p=128),
              xbuf[:].rearrange("p (g d) -> p g d", g=NTG), "outd",
              reads=[f"xtg{g}" for g in range(NTG)])
        P.final_tags.append("outd")

        prog.emit()

    return nc, tap_outs


def _consts():
    i = np.arange(C)
    lt2 = ((i[:, None] <= i[None, :]).astype(np.float32)
           - (i <= C // 2 - 1).astype(np.float32)[:, None])
    LT2P = np.zeros((128, 128), np.float32)
    LT2P[0:64, 0:64] = lt2
    LT2P[64:128, 64:128] = lt2
    MIDS = np.zeros((128, 4), np.float32)
    MIDS[0:64, 0] = (i <= C // 2 - 1)
    MIDS[0:64, 1] = (i > C // 2 - 1)
    MIDS[64:128, 2] = (i <= C // 2 - 1)
    MIDS[64:128, 3] = (i > C // 2 - 1)
    MASKP = np.zeros((128, 128), np.float32)
    mut = (i[:, None] <= i[None, :]).astype(np.float32)
    MASKP[0:64, 0:64] = mut
    MASKP[64:128, 64:128] = mut
    cfv = np.zeros((128, 8), np.float32)
    cfv[:, 0] = EPS
    cfv[:, 1] = 1e-6
    cfv[:, 2] = LNSC
    cfv[:, 3] = -1.0
    cfv[:, 4] = 1.0 / DV
    cfv[0:64, 5] = 1.0
    cfv[64:128, 6] = 1.0
    bf = ml_dtypes.bfloat16
    return {
        "cF32": cfv,
        "cIDN": np.eye(128, dtype=bf),
        "cLT2P": LT2P.astype(bf),
        "cMIDS": MIDS.astype(bf),
        "cMASKP": MASKP.astype(bf),
        "cONES1": np.ones((1, 128), bf),
    }


def prep_core_inputs(inputs):
    x = np.ascontiguousarray(np.asarray(inputs["L0_attn_hidden_states"],
                                        np.float32).reshape(8192, D_MODEL))
    shared = dict(_consts())
    ren = {"Wq": "L0_attn_Wq", "Wk": "L0_attn_Wk", "Wv": "L0_attn_Wv",
           "Ww": "L0_attn_Ww", "cq": "L0_attn_cq", "ck": "L0_attn_ck",
           "cv": "L0_attn_cv", "Wf1": "L0_attn_Wf1", "Wf2": "L0_attn_Wf2",
           "Wb": "L0_attn_Wb", "Wg1": "L0_attn_Wg1", "Wg2": "L0_attn_Wg2",
           "Wo": "L0_attn_Wo", "gate_w": "L0_ffn_gate_w",
           "w_gate": "L0_ffn_w_gate", "w_up": "L0_ffn_w_up",
           "w_down": "L0_ffn_w_down", "sw_gate": "L0_ffn_sw_gate",
           "sw_up": "L0_ffn_sw_up", "sw_down": "L0_ffn_sw_down"}
    for k, src in ren.items():
        shared[k] = np.ascontiguousarray(np.asarray(inputs[src], np.float32))
    shared["bg2"] = np.asarray(inputs["L0_attn_bg2"], np.float32).reshape(1, D_MODEL)
    shared["norm_w"] = np.asarray(inputs["L0_attn_norm_w"],
                                  np.float32).reshape(128, 1)
    maps = []
    for c in range(8):
        m = dict(shared)
        r0 = c * T_CORE
        m["xloc"] = x[r0:r0 + T_CORE]
        if c % 4 == 0:
            m["xhalo"] = np.zeros((3, D_MODEL), np.float32)
        else:
            m["xhalo"] = np.ascontiguousarray(x[r0 - 3:r0])
        sm = np.zeros((128, 8), np.float32)
        sm[:, 0:(c % 4)] = 1.0
        sm[:, 4:8] = 1.0 - sm[:, 0:4]
        m["cSMASK"] = sm
        maps.append(m)
    return maps


def kernel(**inputs):
    maps = prep_core_inputs(inputs)
    nc, _ = build_program()
    res = run_bass_kernel_spmd(nc, maps, core_ids=list(range(8)))
    out = np.empty((8192, D_MODEL), np.float32)
    for c in range(8):
        out[c * T_CORE:(c + 1) * T_CORE] = res.results[c]["out"]
    return out.reshape(2, 4096, D_MODEL)


# version 09 (final)
# speedup vs baseline: 1.0068x; 1.0068x over previous
import numpy as np
import ml_dtypes
import concourse.bass as bass
import concourse.mybir as mybir
from concourse.alu_op_type import AluOpType
from concourse.bass_utils import run_bass_kernel_spmd

F32 = mybir.dt.float32
BF16 = mybir.dt.bfloat16
AF = mybir.ActivationFunctionType
OP = AluOpType

H, DK, DV, M, KW = 16, 128, 128, 128, 4
D_MODEL = 2048
T_CORE = 1024
C = 64
NPAIR = 8
NTG = 8
HPG = 4
NHG = 4
NCH = 16
SCALE = DK ** -0.5
LNSC = float(np.log(SCALE))
EPS = 1e-5
RG = [[0, 1, 2, 3], [4, 5, 6, 7]]
MOE_E = 16
MOE_IC = 11


class Prog:
    ENGINES = ("tensor", "vector", "scalar", "gpsimd", "sync")
    TINY = {"stat", "fws", "combine", "dbgs",
            "vw0", "vw1", "vw2", "vw3", "ev0", "ev1", "ev2", "ev3"}

    def __init__(self, nc):
        self.nc = nc
        self.ops = {e: [] for e in self.ENGINES}
        self.res = {}
        self.dmatags = {}
        self.final_tags = []
        self.lastw = {e: set() for e in self.ENGINES}

    def _st(self, name):
        return self.res.setdefault(name, {"w": None, "rs": []})

    def op(self, engine, fn, reads=(), writes=(), dma_tag=None, extra_deps=()):
        evs = list(extra_deps)
        for r in reads:
            st = self.res.get(r)
            if st is not None and st["w"] is not None:
                evs.append(st["w"])
        for w in writes:
            st = self.res.get(w)
            if st is not None:
                if st["w"] is not None:
                    evs.append(st["w"])
                evs.extend(st["rs"])
        if dma_tag is not None:
            cnt = self.dmatags.get(dma_tag, 0) + 1
            self.dmatags[dma_tag] = cnt
            myev = ("D", dma_tag, cnt)
        else:
            myev = ("E", engine, len(self.ops[engine]))
        best = {}
        for ev in evs:
            if ev[0] == "E" and ev[1] == engine:
                continue
            k = (ev[0], ev[1])
            if k not in best or best[k] < ev[2]:
                best[k] = ev[2]
        deps = [(k[0], k[1], v) for k, v in best.items()]
        if engine in ("vector", "scalar") and dma_tag is None:
            hazard = (set(reads) | set(writes)) & self.lastw[engine] & self.TINY
            if hazard:
                self.ops[engine].append({"fn": lambda eng: eng.drain(),
                                         "deps": [], "ev": None, "need": False})
                if myev[0] == "E":
                    myev = ("E", engine, len(self.ops[engine]))
        self.lastw[engine] = set(writes)
        self.ops[engine].append({"fn": fn, "deps": deps, "ev": myev, "need": False})
        for r in reads:
            self._st(r)["rs"].append(myev)
        for w in writes:
            st = self._st(w)
            st["w"] = myev
            st["rs"] = []
        return myev

    def inherit(self, news, olds):
        for new in news:
            st = self._st(new)
            for o in olds:
                os_ = self.res.get(o)
                if os_ is None:
                    continue
                if os_["w"] is not None:
                    st["rs"].append(os_["w"])
                st["rs"].extend(os_["rs"])

    def emit(self):
        nc = self.nc
        for e in self.ENGINES:
            for o in self.ops[e]:
                for d in o["deps"]:
                    if d[0] == "E":
                        self.ops[d[1]][d[2]]["need"] = True
        rank = {}
        for e in self.ENGINES:
            r = 0
            for i, o in enumerate(self.ops[e]):
                if o["need"]:
                    r += 1
                rank[(e, i)] = r
        import contextlib
        stack = contextlib.ExitStack()
        esem = {e: stack.enter_context(nc.semaphore(f"s_{e}")) for e in self.ENGINES}
        dsem = {t: stack.enter_context(nc.semaphore(f"d_{i}"))
                for i, t in enumerate(self.dmatags)}
        ops, ftags, dtags = self.ops, self.final_tags, self.dmatags
        with stack:
            with nc.Block() as block:
                def make_section(e):
                    def section(eng):
                        hw = {}
                        for o in ops[e]:
                            for d in o["deps"]:
                                if d[0] == "E":
                                    sem, val = esem[d[1]], rank[(d[1], d[2])]
                                else:
                                    sem, val = dsem[d[1]], 16 * d[2]
                                key = (d[0], d[1])
                                if hw.get(key, -1) >= val:
                                    continue
                                eng.wait_ge(sem, val)
                                hw[key] = val
                            inst = o["fn"](eng)
                            if o["ev"] is None:
                                continue
                            if o["ev"][0] == "D":
                                inst.then_inc(dsem[o["ev"][1]], 16)
                            elif o["need"]:
                                inst.then_inc(esem[e], 1)
                        if e == "sync":
                            for t in ftags:
                                if t in dtags:
                                    eng.wait_ge(dsem[t], 16 * dtags[t])
                    return section
                for e in self.ENGINES:
                    getattr(block, e)(make_section(e))

    def dma(self, queue, out, in_, tag, reads=(), writes=(), extra_deps=()):
        return self.op(queue, lambda eng: eng.dma_start(out=out, in_=in_),
                       reads=reads, writes=writes, dma_tag=tag, extra_deps=extra_deps)

    def mm(self, out, lhsT, rhs, start, stop, reads=(), writes=()):
        return self.op("tensor",
                       lambda eng: eng.matmul(out, lhsT, rhs, start=start, stop=stop),
                       reads=reads, writes=writes)

    def transp(self, out, in_, ident, reads=(), writes=()):
        return self.op("tensor", lambda eng: eng.transpose(out, in_, ident),
                       reads=reads, writes=writes)

    def tt(self, engine, out, a, b, alu, reads=(), writes=()):
        return self.op(engine, lambda eng: eng.tensor_tensor(out, a, b, alu),
                       reads=reads, writes=writes)

    def ts(self, engine, out, a, s1, op0, s2=None, op1=None, reads=(), writes=()):
        if s2 is None:
            return self.op(engine, lambda eng: eng.tensor_scalar(out, a, s1, None, op0),
                           reads=reads, writes=writes)
        return self.op(engine, lambda eng: eng.tensor_scalar(out, a, s1, s2, op0, op1),
                       reads=reads, writes=writes)

    def stt(self, engine, out, a, s, b, op0, op1, reads=(), writes=()):
        return self.op(engine,
                       lambda eng: eng.scalar_tensor_tensor(out, a, s, b, op0, op1),
                       reads=reads, writes=writes)

    def act(self, out, in_, func, bias=0.0, scale=1.0, accum=None, reads=(), writes=()):
        if accum is None:
            return self.op("scalar",
                           lambda eng: eng.activation(out, in_, func, bias=bias,
                                                      scale=scale),
                           reads=reads, writes=writes)
        return self.op("scalar",
                       lambda eng: eng.activation(out, in_, func, bias=bias, scale=scale,
                                                  accum_out=accum),
                       reads=reads, writes=writes)

    def red(self, out, in_, alu, negate=False, reads=(), writes=()):
        return self.op("vector",
                       lambda eng: eng.tensor_reduce(out, in_, mybir.AxisListType.X, alu,
                                                     negate=negate),
                       reads=reads, writes=writes)

    def cp(self, engine, out, in_, reads=(), writes=()):
        if engine == "scalar":
            return self.op(engine, lambda eng: eng.copy(out, in_),
                           reads=reads, writes=writes)
        return self.op(engine, lambda eng: eng.tensor_copy(out, in_),
                       reads=reads, writes=writes)

    def recip(self, ap, reads=(), writes=()):
        return self.op("vector", lambda eng: eng.reciprocal(ap, ap),
                       reads=reads, writes=writes)


INTERLEAVE = True
B_LEVEL = 9
LOWCOLS = False
SKIP_SCAN = False
SKIP_AG = False
SKIP_D = False


def drive(gens):
    if not INTERLEAVE:
        for g in gens:
            for _ in g:
                pass
        return
    alive = list(gens)
    while alive:
        nxt = []
        for g in alive:
            try:
                next(g)
                next(g)
                nxt.append(g)
            except StopIteration:
                pass
        alive = nxt


def build_program(taps=()):
    nc = bass.Bass("TRN2", target_bir_lowering=False, debug=False)
    taps = set(taps)

    dx = nc.dram_tensor("xloc", [T_CORE, D_MODEL], F32, kind="ExternalInput").ap()
    dhalo = nc.dram_tensor("xhalo", [3, D_MODEL], F32, kind="ExternalInput").ap()
    wd = {}
    for nm, shp in [("Wq", [D_MODEL, D_MODEL]), ("Wk", [D_MODEL, D_MODEL]),
                    ("Wv", [D_MODEL, D_MODEL]), ("Ww", [D_MODEL, D_MODEL]),
                    ("cq", [D_MODEL, KW]), ("ck", [D_MODEL, KW]), ("cv", [D_MODEL, KW]),
                    ("Wf1", [D_MODEL, DV]), ("Wf2", [DV, D_MODEL]),
                    ("Wb", [D_MODEL, H]), ("Wg1", [D_MODEL, DV]), ("Wg2", [DV, D_MODEL]),
                    ("bg2", [1, D_MODEL]), ("norm_w", [128, 1]), ("Wo", [D_MODEL, D_MODEL]),
                    ("gate_w", [MOE_E, D_MODEL]),
                    ("w_gate", [MOE_E, D_MODEL, 1408]), ("w_up", [MOE_E, D_MODEL, 1408]),
                    ("w_down", [MOE_E, 1408, D_MODEL]),
                    ("sw_gate", [D_MODEL, 2816]), ("sw_up", [D_MODEL, 2816]),
                    ("sw_down", [2816, D_MODEL])]:
        wd[nm] = nc.dram_tensor(nm, shp, F32, kind="ExternalInput").ap()
    dIDN = nc.dram_tensor("cIDN", [128, 128], BF16, kind="ExternalInput").ap()
    dLT2P = nc.dram_tensor("cLT2P", [128, 128], BF16, kind="ExternalInput").ap()
    dMIDS = nc.dram_tensor("cMIDS", [128, 4], BF16, kind="ExternalInput").ap()
    dMASKP = nc.dram_tensor("cMASKP", [128, 128], BF16, kind="ExternalInput").ap()
    dONES1 = nc.dram_tensor("cONES1", [1, 128], BF16, kind="ExternalInput").ap()
    dSMASK = nc.dram_tensor("cSMASK", [128, 8], F32, kind="ExternalInput").ap()
    dCF = nc.dram_tensor("cF32", [128, 8], F32, kind="ExternalInput").ap()
    dout = nc.dram_tensor("out", [T_CORE, D_MODEL], F32, kind="ExternalOutput").ap()
    dspill = nc.dram_tensor("xspill", [T_CORE, D_MODEL], F32, kind="Internal").ap()
    dcc_in = [nc.dram_tensor(f"cc_in{g}", [HPG * 128, 257], F32, kind="Internal").ap()
              for g in range(NHG)]
    dcc_out = [nc.dram_tensor(f"cc_out{g}", [4 * HPG * 128, 257], F32,
                              kind="Internal", addr_space="Local").ap()
               for g in range(NHG)]

    prog = Prog(nc)
    P = prog
    tap_outs = {}

    def mktap(name, src_ap, shape, dtype, reads):
        t = nc.dram_tensor("tap_" + name, shape, dtype, kind="ExternalOutput").ap()
        tap_outs[name] = t
        tag = "tp" + name
        P.dma("sync", t, src_ap, tag, reads=reads)
        P.final_tags.append(tag)

    import contextlib
    stack = contextlib.ExitStack()
    SB = lambda nm, shp, dt: stack.enter_context(nc.sbuf_tensor(nm, shp, dt))
    PS = lambda nm, shp, dt: stack.enter_context(nc.psum_tensor(nm, shp, dt))

    with stack:
        xbuf = SB("xbuf", [128, NTG * D_MODEL], F32)
        combine = SB("combine", [128, NTG * MOE_E], F32)
        statv = SB("statv", [128, 8], F32)
        vws = SB("vws", [128, 64], F32)
        fws = SB("fws", [128, 64], F32)
        hbuf = SB("hbuf", [128, NCH * 1027], BF16)
        sar = SB("sharena", [128, 53312], BF16)
        g1b = SB("g1b", [128, T_CORE], BF16)
        gt1b = SB("gt1b", [128, T_CORE], BF16)
        wbt = SB("wbt", [128, NCH * H], BF16)
        gwt = SB("gwt", [128, NCH * MOE_E], BF16)
        nwb = SB("nwb", [128, 1], F32)
        cIDN = SB("kIDN", [128, 128], BF16)
        cLT2P = SB("kLT2P", [128, 128], BF16)
        cMIDS = SB("kMIDS", [128, 4], BF16)
        cMASKP = SB("kMASKP", [128, 128], BF16)
        cONES1 = SB("kONES1", [1, 128], BF16)
        cSMASK = SB("kSMASK", [128, 8], F32)
        cf = SB("kCF", [128, 8], F32)
        dbgs = SB("dbgs", [128, 80], F32)

        OF_Q, OF_K, OF_UTM, OF_KTM, OF_VTM = 0, 4096, 8192, 12288, 16384
        OF_GTM = 20480
        OF_WS1, OF_WS2 = 24576, 26624
        OF_WSL = 28672
        OF_HQ = 32768
        OF_WS0 = 34880
        OF_OFM = 36928
        EO_WO = 0
        FF_A = 0
        FF_WD = 11264
        FF_GWS = 11264
        FF_WGU = 33792
        FF_WS = 41984

        XB_AGR, XB_STZ, XB_EV = 0, 4112, 5140
        XB_CONV = 6420
        XB_XH, XB_BETA, XB_CC = 8980, 11028, 11156
        XB_WSLB, XB_SG = 11348, 13396
        agrecv = xbuf[:, XB_AGR:XB_AGR + 4112]
        stz = xbuf[:, XB_STZ:XB_STZ + 1028]
        xh = xbuf[0:3, XB_XH:XB_XH + 2048]
        betab = xbuf[:, XB_BETA:XB_BETA + 128]
        ccb = {k: xbuf[:, XB_CC + 64 * i: XB_CC + 64 * (i + 1)]
               for i, k in enumerate(("cq", "ck", "cv"))}
        wslB = xbuf[:, XB_WSLB:XB_WSLB + 2048].bitcast(BF16)
        sgv = xbuf[:, XB_SG:XB_SG + 2048].bitcast(BF16)
        XRES = (["agrecv", "evec", "conv0", "conv1", "xh",
                 "betab", "cccq", "ccck", "cccv", "wslB", "sgate"]
                + [f"st{h}" for h in range(HPG)]
                + [f"ev{h}" for h in range(HPG)]
                + [f"xw{h}" for h in range(HPG)])

        wf1t = sar[:, OF_WSL:OF_WSL + 2048]
        wg1t = sar[:, OF_WSL + 2048:OF_WSL + 4096]
        wf2sl = sar[:, OF_WS0:OF_WS0 + 512]
        wg2sl = sar[:, OF_WS0 + 512:OF_WS0 + 1024]
        bg2sl = sar[0:1, OF_WS0 + 1024:OF_WS0 + 1536]
        gwsb = sar[0:MOE_E, FF_GWS:FF_GWS + 2048]
        WSBASE = {0: OF_WS0, 1: OF_WS1, 2: OF_WS2, 3: OF_HQ}

        def wsb(hh, i):
            return sar[:, WSBASE[hh] + i * 128: WSBASE[hh] + (i + 1) * 128]

        def xws(hh, j):
            return xbuf[:, XB_CONV + (hh * 4 + j) * 128:
                        XB_CONV + (hh * 4 + j + 1) * 128]

        WN = {hh: f"w{hh}" for hh in range(HPG)}

        pb = [PS(f"pb{i}", [128, 512], F32) for i in range(4)]
        pt = [PS(f"pt{i}", [128, 1024], BF16) for i in range(4)]
        ptf = [t[:].bitcast(F32) for t in pt]
        PBN = [f"pb{i}" for i in range(4)]
        PTN = [f"pt{i}" for i in range(4)]

        P.dma("sync", cIDN[:], dIDN, "cIDN", writes=["cIDN"])
        P.dma("sync", cLT2P[:], dLT2P, "cLT2P", writes=["cLT2P"])
        P.dma("sync", cMIDS[:], dMIDS, "cMIDS", writes=["cMIDS"])
        P.dma("sync", cMASKP[:], dMASKP, "cMASKP", writes=["cMASKP"])
        P.dma("sync", cONES1[:], dONES1, "cONES1", writes=["cONES1"])
        P.dma("sync", cSMASK[:], dSMASK, "cSMASK", writes=["cSMASK"])
        P.dma("sync", cf[:], dCF, "cf", writes=["cf"])
        P.dma("sync", nwb[:], wd["norm_w"], "nwb", writes=["nwb"])
        P.dma("gpsimd", wf1t.rearrange("p (c v) -> p c v", c=NCH),
              wd["Wf1"].rearrange("(c p) v -> p c v", p=128), "wf1", writes=["wf1t"])
        P.dma("gpsimd", wg1t.rearrange("p (c v) -> p c v", c=NCH),
              wd["Wg1"].rearrange("(c p) v -> p c v", p=128), "wg1", writes=["wg1t"])
        P.dma("gpsimd", wbt[:].rearrange("p (c e) -> p c e", c=NCH),
              wd["Wb"].rearrange("(c p) e -> p c e", p=128), "wb", writes=["wbt"])
        P.dma("sync", xbuf[:].rearrange("p (g d) -> p g d", g=NTG),
              dx.rearrange("(g p) d -> p g d", p=128), "x",
              writes=[f"xtg{g}" for g in range(NTG)])

        lncnt = [0]

        def layernorm_tm(src_res, src_ap, dst_ap, dst_res, rows=128):
            s_ = statv
            junk_ap = sar[0:rows, OF_OFM:OF_OFM + D_MODEL]
            P.red(s_[0:rows, 0:1], src_ap, OP.add, reads=[src_res], writes=["stat"])
            P.act(junk_ap, src_ap, AF.Square, accum=s_[0:rows, 1:2],
                  reads=[src_res], writes=["stat", "junkws"])
            P.ts("vector", s_[0:rows, 2:3], s_[0:rows, 0:1], 1.0 / D_MODEL, OP.mult,
                 reads=["stat"], writes=["stat"])
            P.ts("vector", s_[0:rows, 4:5], s_[0:rows, 1:2], 1.0 / D_MODEL, OP.mult,
                 reads=["stat"], writes=["stat"])
            P.tt("vector", s_[0:rows, 3:4], s_[0:rows, 2:3], s_[0:rows, 2:3], OP.mult,
                 reads=["stat"], writes=["stat"])
            P.tt("vector", s_[0:rows, 4:5], s_[0:rows, 4:5], s_[0:rows, 3:4],
                 OP.subtract, reads=["stat"], writes=["stat"])
            P.act(s_[0:rows, 5:6], s_[0:rows, 4:5], AF.Sqrt, bias=cf[0:rows, 0:1],
                  reads=["stat", "cf"], writes=["stat"])
            P.recip(s_[0:rows, 5:6], reads=["stat"], writes=["stat"])
            P.ts("vector", s_[0:rows, 6:7], s_[0:rows, 2:3], s_[0:rows, 5:6], OP.mult,
                 reads=["stat"], writes=["stat"])
            P.ts("vector", s_[0:rows, 7:8], s_[0:rows, 6:7], -1.0, OP.mult,
                 reads=["stat"], writes=["stat"])
            P.ts("vector", dst_ap, src_ap, s_[0:rows, 5:6], OP.mult,
                 s_[0:rows, 7:8], OP.add, reads=[src_res, "stat"], writes=[dst_res])
            if "dbgs" in taps and lncnt[0] < 10:
                i = lncnt[0]
                P.cp("vector", dbgs[0:rows, i * 8:(i + 1) * 8], s_[0:rows, 0:8],
                     reads=["stat"], writes=["dbgs"])
            lncnt[0] += 1

        def htg_ap(i):
            return sar[:, OF_GTM + i * D_MODEL: OF_GTM + (i + 1) * D_MODEL]

        for tg in range(NTG):
            sl = htg_ap(tg % 2)
            layernorm_tm(f"xtg{tg}", xbuf[:, tg * D_MODEL:(tg + 1) * D_MODEL], sl,
                         f"htg{tg % 2}")
            for c in range(NCH):
                ptile = pt[c % 2]
                P.transp(ptile[:, 0:128], sl[:, c * 128:(c + 1) * 128], cIDN[:],
                         reads=[f"htg{tg % 2}", "cIDN"], writes=[PTN[c % 2]])
                P.cp("vector",
                     hbuf[:, c * 1027 + 3 + tg * 128: c * 1027 + 3 + (tg + 1) * 128],
                     ptile[:, 0:128], reads=[PTN[c % 2]], writes=["hbuf"])
        spill_ev = P.dma("sync", dspill.rearrange("(g p) d -> p g d", p=128),
                         xbuf[:].rearrange("p (g d) -> p g d", g=NTG), "spill",
                         reads=[f"xtg{g}" for g in range(NTG)])
        P.inherit(XRES, [f"xtg{g}" for g in range(NTG)])
        P.dma("sync", xh, dhalo, "xh", writes=["xh"])
        layernorm_tm("xh", xh, htg_ap(0)[0:3, 0:D_MODEL], "htg0", rows=3)
        for c in range(NCH):
            ptile = pt[c % 2]
            P.transp(ptile[0:128, 0:3], htg_ap(0)[0:3, c * 128:(c + 1) * 128],
                     cIDN[0:3, 0:3], reads=["htg0", "cIDN"], writes=[PTN[c % 2]])
            P.cp("vector", hbuf[:, c * 1027: c * 1027 + 3], ptile[:, 0:3],
                 reads=[PTN[c % 2]], writes=["hbuf"])
        for k in ("cq", "ck", "cv"):
            P.dma("sync", ccb[k].rearrange("p (c j) -> p c j", c=NCH),
                  wd[k].rearrange("(c p) j -> p c j", p=128), "cc" + k,
                  writes=["cc" + k])

        if "dbgs" in taps:
            mktap("dbgs", dbgs[:], [128, 80], F32, ["dbgs"])
        if "hbuf" in taps:
            mktap("hbuf", hbuf[:], [128, NCH * 1027], BF16, ["hbuf"])

        def proj_fm(dst_ap, dst_res, wtiles, wres, psum_i):
            for w_i, base in enumerate((3, 515)):
                ps = pb[(psum_i + w_i) % 4]
                psn = PBN[(psum_i + w_i) % 4]
                for kc in range(NCH):
                    P.mm(ps[:, 0:512], wtiles[:, kc * 128:(kc + 1) * 128],
                         hbuf[:, kc * 1027 + base: kc * 1027 + base + 512],
                         start=(kc == 0), stop=(kc == NCH - 1),
                         reads=[wres, "hbuf"], writes=[psn])
                P.cp("vector", dst_ap[:, w_i * 512:(w_i + 1) * 512], ps[:, 0:512],
                     reads=[psn], writes=[dst_res])
        proj_fm(g1b[:], "g1b", wf1t, "wf1t", 0)
        proj_fm(gt1b[:], "gt1b", wg1t, "wg1t", 2)
        for tg in range(NTG):
            ps = ptf[tg % 2]
            psn = PTN[tg % 2]
            for kc in range(NCH):
                P.mm(ps[:, 0:H],
                     hbuf[:, kc * 1027 + 3 + tg * 128: kc * 1027 + 3 + (tg + 1) * 128],
                     wbt[:, kc * H:(kc + 1) * H],
                     start=(kc == 0), stop=(kc == NCH - 1),
                     reads=["hbuf", "wbt"], writes=[psn])
            P.act(betab[:, tg * H:(tg + 1) * H], ps[:, 0:H], AF.Sigmoid,
                  reads=[psn], writes=["betab"])
        if "beta" in taps:
            mktap("beta", betab, [128, NTG * H], F32, ["betab"])

        P.inherit(["gtm"], ["htg0", "htg1"])
        P.inherit(["wslA"], ["wf1t", "wg1t"])
        P.inherit(["ofm"], ["junkws"])

        def phase_A(hg):
            P.inherit(["wf2sl", "wg2sl", "bg2sl"], [WN[0]])
            P.inherit(["hq0", "hq1"], [WN[3]])
            P.inherit(["conv0", "conv1"], [f"xw{h}" for h in range(HPG)])
            P.dma("gpsimd", wf2sl, wd["Wf2"][:, hg * 512:(hg + 1) * 512], "wf2",
                  writes=["wf2sl"])
            P.dma("gpsimd", wg2sl, wd["Wg2"][:, hg * 512:(hg + 1) * 512], "wg2",
                  writes=["wg2sl"])
            P.dma("gpsimd", bg2sl, wd["bg2"][:, hg * 512:(hg + 1) * 512], "bg2",
                  writes=["bg2sl"])
            for si, (wn, ccn) in enumerate((("Wq", "cq"), ("Wk", "ck"),
                                            ("Wv", "cv"), ("Ww", "cv"))):
                ofs = (OF_Q, OF_K, OF_UTM, OF_KTM)[si]
                sres = ("q4", "k4", "vr4", "wr4")[si]
                P.dma("gpsimd",
                      sar[:, OF_WSL: OF_WSL + 4096].rearrange("p (c n) -> p c n", c=8),
                      wd[wn][0:1024, hg * 512:(hg + 1) * 512]
                      .rearrange("(c p) n -> p c n", p=128),
                      "wslA", writes=["wslA"])
                P.dma("gpsimd", wslB.rearrange("p (c n) -> p c n", c=8),
                      wd[wn][1024:2048, hg * 512:(hg + 1) * 512]
                      .rearrange("(c p) n -> p c n", p=128),
                      "wslB", writes=["wslB"])
                for hh in range(HPG):
                    h_abs = hg * HPG + hh
                    hq0 = OF_HQ + (hh % 2) * 1027
                    hqres = f"hq{hh % 2}"
                    for w_i, (base, n) in enumerate(((0, 512), (512, 512), (1024, 3))):
                        ps = pb[w_i % 3]
                        psn = PBN[w_i % 3]
                        for kc in range(NCH):
                            if kc < 8:
                                wtile = sar[:, OF_WSL + kc * 512 + hh * 128:
                                            OF_WSL + kc * 512 + (hh + 1) * 128]
                                wres = "wslA"
                            else:
                                wtile = wslB[:, (kc - 8) * 512 + hh * 128:
                                             (kc - 8) * 512 + (hh + 1) * 128]
                                wres = "wslB"
                            P.mm(ps[:, 0:n], wtile,
                                 hbuf[:, kc * 1027 + base: kc * 1027 + base + n],
                                 start=(kc == 0), stop=(kc == NCH - 1),
                                 reads=[wres, "hbuf"], writes=[psn])
                        P.act(sar[:, hq0 + base: hq0 + base + n], ps[:, 0:n],
                              AF.Copy, reads=[psn], writes=[hqres])
                    cc = ccb[ccn]
                    cacc = xbuf[:, XB_CONV + (hh % 2) * 1024:
                                XB_CONV + (hh % 2 + 1) * 1024]
                    caccres = f"conv{hh % 2}"
                    hq = sar[:, hq0: hq0 + 1027]
                    P.ts("vector", cacc, hq[:, 3:1027],
                         cc[:, h_abs * KW + 3: h_abs * KW + 4], OP.mult,
                         reads=[hqres, "cc" + ccn], writes=[caccres])
                    for j in (1, 2, 3):
                        P.stt("vector", cacc, hq[:, 3 - j: 3 - j + 1024],
                              cc[:, h_abs * KW + 3 - j: h_abs * KW + 4 - j], cacc,
                              OP.mult, OP.add, reads=[hqres, "cc" + ccn, caccres],
                              writes=[caccres])
                    dst = sar[:, ofs + hh * 1024: ofs + (hh + 1) * 1024]
                    P.act(dst, cacc, AF.Silu, reads=[caccres], writes=[sres])
            for tg in range(NTG):
                ps = pb[3] if tg % 2 == 0 else ptf[0]
                psn = PBN[3] if tg % 2 == 0 else PTN[0]
                P.mm(ps[:, 0:512], g1b[:, tg * 128:(tg + 1) * 128], wf2sl,
                     start=True, stop=True, reads=["g1b", "wf2sl"], writes=[psn])
                P.act(sar[:, OF_GTM + tg * 512: OF_GTM + (tg + 1) * 512], ps[:, 0:512],
                      AF.Sigmoid, reads=[psn], writes=["gtm"])
            for tg in range(NTG):
                gsl = sar[:, OF_GTM + tg * 512: OF_GTM + (tg + 1) * 512]
                P.act(gsl, gsl, AF.Ln, reads=["gtm"], writes=["gtm"])
            for tg in range(NTG):
                ps = pb[3] if tg % 2 == 0 else ptf[0]
                psn = PBN[3] if tg % 2 == 0 else PTN[0]
                P.mm(ps[:, 0:512], gt1b[:, tg * 128:(tg + 1) * 128], wg2sl,
                     start=True, stop=False, reads=["gt1b", "wg2sl"], writes=[psn])
                P.mm(ps[:, 0:512], cONES1[:, 0:128], bg2sl,
                     start=False, stop=True, reads=["cONES1", "bg2sl"], writes=[psn])
                P.act(sgv[:, tg * 512:(tg + 1) * 512], ps[:, 0:512],
                      AF.Sigmoid, reads=[psn], writes=["sgate"])

        def bchain(hg, hh):
            bank, bkn = pb[hh], PBN[hh]
            ptb, ptn = pt[hh], PTN[hh]
            wn = WN[hh]
            vwn, evn, stn, xwn = f"vw{hh}", f"ev{hh}", f"st{hh}", f"xw{hh}"
            vw = lambda i: vws[:, hh * 16 + i: hh * 16 + i + 1]
            skz = stz[:, hh * 257: hh * 257 + 128]
            svz = stz[:, hh * 257 + 128: hh * 257 + 256]
            gslot = stz[:, hh * 257 + 256: hh * 257 + 257]
            for p in range(NPAIR):
                h_abs = hg * HPG + hh
                gsl = sar[:, OF_GTM + p * 512 + hh * 128:
                          OF_GTM + p * 512 + (hh + 1) * 128]
                evs = xbuf[:, XB_EV + (hh * NPAIR + p) * 8:
                           XB_EV + (hh * NPAIR + p) * 8 + 8]
                utm = sar[:, OF_UTM + hh * 1024 + p * 128:
                          OF_UTM + hh * 1024 + (p + 1) * 128]
                ktm = sar[:, OF_KTM + hh * 1024 + p * 128:
                          OF_KTM + hh * 1024 + (p + 1) * 128]
                vtm = sar[:, OF_VTM + hh * 1024 + p * 128:
                          OF_VTM + hh * 1024 + (p + 1) * 128]
                ksl = sar[:, OF_K + hh * 1024 + p * 128:
                          OF_K + hh * 1024 + (p + 1) * 128]
                vraw = utm
                wraw = ktm
                bcol = betab[:, p * H + h_abs: p * H + h_abs + 1]
                P.mm(bank[:, 0:128], cLT2P[:], gsl, start=True, stop=True,
                     reads=["cLT2P", "gtm"], writes=[bkn]); yield
                P.mm(bank[:, 128:132], gsl, cMIDS[:], start=True, stop=True,
                     reads=["gtm", "cMIDS"], writes=[bkn]); yield
                P.act(wsb(hh, 0), bank[:, 0:128], AF.Exp, scale=cf[:, 3:4],
                      reads=[bkn, "cf"], writes=[wn]); yield
                P.act(evs[:, 0:4], bank[:, 128:132], AF.Exp, reads=[bkn],
                      writes=[evn]); yield
                P.tt("vector", evs[:, 4:5], evs[:, 0:1], evs[:, 1:2], OP.mult,
                     reads=[evn], writes=[evn]); yield
                P.tt("vector", evs[:, 5:6], evs[:, 2:3], evs[:, 3:4], OP.mult,
                     reads=[evn], writes=[evn]); yield
                P.tt("vector", evs[:, 6:7], evs[:, 5:6], evs[:, 1:2], OP.mult,
                     reads=[evn], writes=[evn]); yield
                P.transp(ptb[:, 0:128], vraw, cIDN[:], reads=["vr4", "cIDN"],
                         writes=[ptn]); yield
                P.cp("scalar", vtm, ptb[:, 0:128], reads=[ptn], writes=["vtm4"]); yield
                P.transp(ptb[:, 128:256], wraw, cIDN[:], reads=["wr4", "cIDN"],
                         writes=[ptn]); yield
                P.cp("scalar", wsb(hh, 1), ptb[:, 128:256], reads=[ptn],
                     writes=[wn]); yield
                P.act(wsb(hh, 2), ptb[:, 128:256], AF.Square, accum=vw(0),
                      reads=[ptn], writes=[wn, vwn]); yield
                P.act(vw(1), vw(0), AF.Sqrt, bias=cf[:, 1:2],
                      reads=[vwn, "cf"], writes=[vwn]); yield
                P.recip(vw(1), reads=[vwn], writes=[vwn]); yield
                P.stt("vector", utm, wsb(hh, 1), vw(1), wsb(hh, 0), OP.mult, OP.mult,
                      reads=[wn, vwn, "vr4"], writes=["utm4", "vr4"]); yield
                P.transp(ptb[:, 256:384], ksl, cIDN[:], reads=["k4", "cIDN"],
                         writes=[ptn]); yield
                P.cp("scalar", ktm, ptb[:, 256:384], reads=[ptn, "wr4"],
                     writes=["ktm4", "wr4"]); yield
                P.ts("vector", wsb(hh, 3), utm, bcol, OP.mult, cf[:, 5:6], OP.mult,
                     reads=["utm4", "betab", "cf"], writes=[wn]); yield
                P.ts("vector", wsb(hh, 4), utm, bcol, OP.mult, cf[:, 6:7], OP.mult,
                     reads=["utm4", "betab", "cf"], writes=[wn]); yield
                ubA, ubB = wsb(hh, 3), wsb(hh, 4)
                P.mm(bank[:, 0:128], ubA, ktm, start=True,
                     stop=True, reads=[wn, "ktm4"], writes=[bkn]); yield
                P.mm(bank[:, 128:256], ubA, vtm, start=True,
                     stop=True, reads=[wn, "vtm4"], writes=[bkn]); yield
                P.mm(bank[:, 256:384], ubB, ktm, start=True,
                     stop=True, reads=[wn, "ktm4"], writes=[bkn]); yield
                P.mm(bank[:, 384:512], ubB, vtm, start=True,
                     stop=True, reads=[wn, "vtm4"], writes=[bkn]); yield
                P.tt("vector", vw(2), evs[:, 4:5], evs[:, 5:6], OP.mult,
                     reads=[evn], writes=[vwn]); yield
                for st_i, ssl in ((0, skz), (1, svz)):
                    tws = xws(hh, 3)
                    P.ts("vector", tws, bank[:, st_i * 128:(st_i + 1) * 128],
                         evs[:, 6:7], OP.mult, reads=[bkn, evn],
                         writes=[xwn]); yield
                    P.stt("vector", tws, bank[:, 256 + st_i * 128: 384 + st_i * 128],
                          evs[:, 3:4], tws, OP.mult, OP.add,
                          reads=[bkn, evn, xwn], writes=[xwn]); yield
                    P.stt("vector", ssl, ssl, vw(2), tws, OP.mult, OP.add,
                          reads=[vwn, xwn], writes=[stn]); yield
                if p == 0:
                    P.ts("vector", gslot, vw(2), 1.0, OP.mult,
                         reads=[vwn], writes=[stn]); yield
                else:
                    P.tt("vector", gslot, gslot, vw(2), OP.mult,
                         reads=[vwn], writes=[stn]); yield

        def phase_AG(hg):
            P.dma("sync", dcc_in[hg].rearrange("(h p) c -> p h c", p=128),
                  stz.rearrange("p (h c) -> p h c", h=HPG), f"agi{hg}",
                  reads=[f"st{h}" for h in range(HPG)], writes=[f"ccin{hg}"])
            P.op("gpsimd",
                 lambda eng: eng.collective_compute(
                     "AllGather", OP.bypass, replica_groups=RG,
                     ins=[dcc_in[hg]], outs=[dcc_out[hg]]),
                 reads=[f"ccin{hg}"], writes=[f"ccout{hg}"])
            P.dma("sync", agrecv.rearrange("p (r c) -> p r c", r=4 * HPG),
                  dcc_out[hg].rearrange("(r p) c -> p r c", p=128), f"ago{hg}",
                  reads=[f"ccout{hg}"], writes=["agrecv"])
            for r in range(4):
                for hh in range(HPG):
                    base = (r * HPG + hh) * 257
                    gr = agrecv[:, base + 256: base + 257]
                    mcol = cSMASK[:, r: r + 1]
                    omcol = cSMASK[:, 4 + r: 5 + r]
                    gh = vws[:, hh * 16 + 10: hh * 16 + 11]
                    vwn, stn, xwn = f"vw{hh}", f"st{hh}", f"xw{hh}"
                    if r > 0:
                        P.ts("vector", gh, gr, mcol, OP.mult,
                             reads=["agrecv", "cSMASK"], writes=[vwn])
                        P.tt("vector", gh, gh, omcol, OP.add,
                             reads=["cSMASK"], writes=[vwn])
                    for st_i in range(2):
                        ssl = stz[:, hh * 257 + st_i * 128:
                                  hh * 257 + (st_i + 1) * 128]
                        srr = agrecv[:, base + st_i * 128: base + (st_i + 1) * 128]
                        u1 = xws(hh, 3)
                        P.ts("vector", u1, srr, mcol, OP.mult,
                             reads=["agrecv", "cSMASK"], writes=[xwn])
                        if r == 0:
                            P.cp("vector", ssl, u1, reads=[xwn], writes=[stn])
                        else:
                            P.stt("vector", ssl, ssl, gh, u1, OP.mult, OP.add,
                                  reads=[vwn, xwn], writes=[stn])

        def dchain(hg, hh):
            bank, bkn = pb[hh], PBN[hh]
            ptb, ptn = pt[hh], PTN[hh]
            wn = WN[hh]
            vwn, evn, stn, xwn = f"vw{hh}", f"ev{hh}", f"st{hh}", f"xw{hh}"
            vw = lambda i: vws[:, hh * 16 + i: hh * 16 + i + 1]
            skz = stz[:, hh * 257: hh * 257 + 128]
            svz = stz[:, hh * 257 + 128: hh * 257 + 256]
            for p in range(NPAIR):
                h_abs = hg * HPG + hh
                gsl = sar[:, OF_GTM + p * 512 + hh * 128:
                          OF_GTM + p * 512 + (hh + 1) * 128]
                evs = xbuf[:, XB_EV + (hh * NPAIR + p) * 8:
                           XB_EV + (hh * NPAIR + p) * 8 + 8]
                utm = sar[:, OF_UTM + hh * 1024 + p * 128:
                          OF_UTM + hh * 1024 + (p + 1) * 128]
                ktm = sar[:, OF_KTM + hh * 1024 + p * 128:
                          OF_KTM + hh * 1024 + (p + 1) * 128]
                vtm = sar[:, OF_VTM + hh * 1024 + p * 128:
                          OF_VTM + hh * 1024 + (p + 1) * 128]
                qsl = sar[:, OF_Q + hh * 1024 + p * 128:
                          OF_Q + hh * 1024 + (p + 1) * 128]
                ksl = sar[:, OF_K + hh * 1024 + p * 128:
                          OF_K + hh * 1024 + (p + 1) * 128]
                bcol = betab[:, p * H + h_abs: p * H + h_abs + 1]
                P.ts("vector", wsb(hh, 0), skz, evs[:, 0:1], OP.mult,
                     reads=[stn, evn], writes=[wn]); yield
                P.ts("vector", wsb(hh, 1), svz, evs[:, 0:1], OP.mult,
                     reads=[stn, evn], writes=[wn]); yield
                P.ts("vector", wsb(hh, 2), utm, bcol, OP.mult, cf[:, 5:6], OP.mult,
                     reads=["utm4", "betab", "cf"], writes=[wn]); yield
                P.ts("vector", wsb(hh, 15), utm, bcol, OP.mult, cf[:, 6:7], OP.mult,
                     reads=["utm4", "betab", "cf"], writes=[wn]); yield
                ubA, ubB = wsb(hh, 2), wsb(hh, 15)
                P.mm(bank[:, 0:128], ubA, ktm, start=True,
                     stop=True, reads=[wn, "ktm4"], writes=[bkn]); yield
                P.mm(bank[:, 128:256], ubA, vtm, start=True,
                     stop=True, reads=[wn, "vtm4"], writes=[bkn]); yield
                for st_i, ssl in ((0, skz), (1, svz)):
                    tws = xws(hh, 3)
                    P.ts("vector", tws, bank[:, st_i * 128:(st_i + 1) * 128],
                         evs[:, 1:2], OP.mult, reads=[bkn, evn],
                         writes=[xwn]); yield
                    P.stt("vector", ssl, ssl, evs[:, 4:5], tws, OP.mult, OP.add,
                          reads=[evn, xwn], writes=[stn]); yield
                P.ts("vector", wsb(hh, 3), skz, evs[:, 2:3], OP.mult,
                     reads=[stn, evn], writes=[wn]); yield
                P.ts("vector", wsb(hh, 4), svz, evs[:, 2:3], OP.mult,
                     reads=[stn, evn], writes=[wn]); yield
                P.mm(bank[:, 256:384], ubB, ktm, start=True,
                     stop=True, reads=[wn, "ktm4"], writes=[bkn]); yield
                P.mm(bank[:, 384:512], ubB, vtm, start=True,
                     stop=True, reads=[wn, "vtm4"], writes=[bkn]); yield
                for st_i, ssl in ((0, skz), (1, svz)):
                    tws = xws(hh, 3)
                    P.ts("vector", tws, bank[:, 256 + st_i * 128: 384 + st_i * 128],
                         evs[:, 3:4], OP.mult, reads=[bkn, evn],
                         writes=[xwn]); yield
                    P.stt("vector", ssl, ssl, evs[:, 5:6], tws, OP.mult, OP.add,
                          reads=[evn, xwn], writes=[stn]); yield
                P.transp(ptb[:, 0:128], wsb(hh, 0), cIDN[:], reads=[wn, "cIDN"],
                         writes=[ptn]); yield
                P.cp("scalar", wsb(hh, 5), ptb[:, 0:128], reads=[ptn],
                     writes=[wn]); yield
                P.transp(ptb[:, 128:256], wsb(hh, 3), cIDN[:], reads=[wn, "cIDN"],
                         writes=[ptn]); yield
                P.cp("scalar", wsb(hh, 6), ptb[:, 128:256], reads=[ptn],
                     writes=[wn]); yield
                P.transp(ptb[:, 256:384], utm, cIDN[:], reads=["utm4", "cIDN"],
                         writes=[ptn]); yield
                P.cp("scalar", wsb(hh, 7), ptb[:, 256:384], reads=[ptn],
                     writes=[wn]); yield
                P.mm(bank[:, 0:128], cLT2P[:], gsl, start=True, stop=True,
                     reads=["cLT2P", "gtm"], writes=[bkn]); yield
                P.act(wsb(hh, 8), bank[:, 0:128], AF.Exp, bias=cf[:, 2:3],
                      reads=[bkn, "cf"], writes=[wn]); yield
                P.mm(bank[:, 128:256], ksl, qsl, start=True, stop=True,
                     reads=["k4", "q4"], writes=[bkn]); yield
                P.ts("vector", wsb(hh, 9), cMASKP[:], bcol, OP.mult,
                     reads=["cMASKP", "betab"], writes=[wn]); yield
                P.tt("vector", wsb(hh, 10), bank[:, 128:256], wsb(hh, 9), OP.mult,
                     reads=[bkn, wn], writes=[wn]); yield
                P.mm(bank[:, 256:384], wsb(hh, 10), utm, start=True, stop=False,
                     reads=[wn, "utm4"], writes=[bkn]); yield
                P.mm(bank[0:64, 256:384], qsl[:, 0:64], wsb(hh, 5), start=False,
                     stop=False, reads=["q4", wn], writes=[bkn]); yield
                P.mm(bank[64:128, 256:384], qsl[:, 64:128], wsb(hh, 6), start=False,
                     stop=True, reads=["q4", wn], writes=[bkn]); yield
                s_sb = xws(hh, 1)
                P.tt("vector", s_sb, bank[:, 256:384], wsb(hh, 8), OP.mult,
                     reads=[bkn, wn], writes=[xwn]); yield
                P.red(vw(4), s_sb, OP.max, negate=True,
                      reads=[xwn], writes=[vwn]); yield
                P.act(wsb(hh, 11), s_sb, AF.Exp, bias=vw(4),
                      reads=[xwn, vwn], writes=[wn]); yield
                P.red(vw(5), wsb(hh, 11), OP.add, reads=[wn], writes=[vwn]); yield
                P.ts("vector", vw(6), vw(5), SCALE, OP.mult,
                     reads=[vwn], writes=[vwn]); yield
                P.recip(vw(6), reads=[vwn], writes=[vwn]); yield
                P.tt("vector", wsb(hh, 12), wsb(hh, 11), wsb(hh, 8), OP.mult,
                     reads=[wn], writes=[wn]); yield
                P.transp(ptb[:, 384:512], wsb(hh, 12), cIDN[:], reads=[wn, "cIDN"],
                         writes=[ptn]); yield
                P.cp("scalar", wsb(hh, 13), ptb[:, 384:512], reads=[ptn],
                     writes=[wn]); yield
                P.mm(bank[:, 384:512], wsb(hh, 7), wsb(hh, 13), start=True,
                     stop=True, reads=[wn], writes=[bkn]); yield
                P.tt("vector", wsb(hh, 14), bank[:, 384:512], wsb(hh, 9), OP.mult,
                     reads=[bkn, wn], writes=[wn]); yield
                P.mm(bank[:, 0:128], wsb(hh, 14), vtm, start=True, stop=False,
                     reads=[wn, "vtm4"], writes=[bkn]); yield
                P.mm(bank[0:64, 0:128], wsb(hh, 13)[:, 0:64], wsb(hh, 1),
                     start=False, stop=False, reads=[wn], writes=[bkn]); yield
                P.mm(bank[64:128, 0:128], wsb(hh, 13)[:, 64:128], wsb(hh, 4),
                     start=False, stop=True, reads=[wn], writes=[bkn]); yield
                o1 = xws(hh, 2)
                P.ts("vector", o1, bank[:, 0:128], vw(6), OP.mult,
                     reads=[bkn, vwn], writes=[xwn]); yield
                P.act(wsb(hh, 15), o1, AF.Square, accum=vw(7),
                      reads=[xwn], writes=[wn, vwn]); yield
                P.act(vw(8), vw(7), AF.Sqrt, bias=cf[:, 0:1], scale=cf[:, 4:5],
                      reads=[vwn, "cf"], writes=[vwn]); yield
                P.recip(vw(8), reads=[vwn], writes=[vwn]); yield
                sg = sgv[:, p * 512 + hh * 128: p * 512 + (hh + 1) * 128]
                P.stt("vector", wsb(hh, 12), o1, vw(8), sg, OP.mult, OP.mult,
                      reads=[xwn, vwn, "sgate"], writes=[wn]); yield
                P.transp(ptb[:, 0:128], wsb(hh, 12), cIDN[:], reads=[wn, "cIDN"],
                         writes=[ptn]); yield
                P.cp("vector", sar[:, OF_OFM + h_abs * 1024 + p * 128:
                                   OF_OFM + h_abs * 1024 + (p + 1) * 128],
                     ptb[:, 0:128], reads=[ptn], writes=["ofm"]); yield

        def phase_B(hg):
            P.inherit([WN[0]], ["wf2sl", "wg2sl", "bg2sl"])
            P.inherit([WN[3]], ["hq0", "hq1"])
            P.inherit([f"xw{h}" for h in range(HPG)], ["conv0", "conv1"])
            for hh in range(HPG):
                P.op("gpsimd",
                     lambda eng, a=stz[:, hh * 257:(hh + 1) * 257]: eng.memset(a, 0.0),
                     writes=[f"st{hh}"])
            drive([bchain(hg, hh) for hh in range(HPG)])

        def phase_D(hg):
            drive([dchain(hg, hh) for hh in range(HPG)])

        for hg in range(NHG):
            phase_A(hg)
            if not SKIP_SCAN:
                phase_B(hg)
                if not SKIP_AG:
                    phase_AG(hg)
                if not SKIP_D:
                    phase_D(hg)
        if SKIP_SCAN or SKIP_D:
            P.op("gpsimd", lambda eng: eng.memset(sar[:, OF_OFM:OF_OFM + 16384], 0.0),
                 reads=["junkws"], writes=["ofm"])

        if "ofm" in taps:
            mktap("ofm", sar[:, OF_OFM:OF_OFM + 16384], [128, H * T_CORE], BF16,
                  ["ofm"])
        if "q4" in taps:
            mktap("q4", sar[:, OF_Q:OF_Q + 4096], [128, HPG * T_CORE], BF16, ["q4"])
        if "gtm" in taps:
            mktap("gtm", sar[:, OF_GTM:OF_GTM + 4096], [128, NTG * 512], BF16,
                  ["gtm"])
        if "sgate" in taps:
            mktap("sgate", sgv[:, 0:4096], [128, NTG * 512], BF16, ["sgate"])
        if "utm" in taps:
            mktap("utm", sar[:, OF_UTM:OF_UTM + 4096], [128, HPG * T_CORE], BF16,
                  ["utm4"])

        P.dma("sync", xbuf[:].rearrange("p (g d) -> p g d", g=NTG),
              dspill.rearrange("(g p) d -> p g d", p=128), "unspill",
              writes=[f"xtg{g}" for g in range(NTG)] + XRES,
              extra_deps=[spill_ev])
        P.inherit(["wo0", "wo1"], ["q4", "k4", "utm4", "ktm4"])
        for dt_ in range(4):
            slot = dt_ % 2
            wosl = sar[:, EO_WO + slot * 8192: EO_WO + (slot + 1) * 8192]
            P.dma("gpsimd", wosl.rearrange("p (c n) -> p c n", c=NCH),
                  wd["Wo"][:, dt_ * 512:(dt_ + 1) * 512]
                  .rearrange("(c p) n -> p c n", p=128),
                  f"wo{slot}", writes=[f"wo{slot}"])
            P.ts("vector", wosl, wosl, nwb[:, 0:1], OP.mult,
                 reads=[f"wo{slot}", "nwb"], writes=[f"wo{slot}"])
            for tg in range(NTG):
                ps = pb[tg % 4]
                psn = PBN[tg % 4]
                for c in range(NCH):
                    P.mm(ps[:, 0:512],
                         sar[:, OF_OFM + c * 1024 + tg * 128:
                             OF_OFM + c * 1024 + (tg + 1) * 128],
                         wosl[:, c * 512:(c + 1) * 512],
                         start=(c == 0), stop=(c == NCH - 1),
                         reads=["ofm", f"wo{slot}"], writes=[psn])
                xsl = xbuf[:, tg * D_MODEL + dt_ * 512: tg * D_MODEL + (dt_ + 1) * 512]
                P.tt("vector", xsl, ps[:, 0:512], xsl, OP.add,
                     reads=[psn, f"xtg{tg}"], writes=[f"xtg{tg}"])
        if "x2" in taps:
            mktap("x2", xbuf[:], [128, NTG * D_MODEL], F32,
                  [f"xtg{g}" for g in range(NTG)])

        P.inherit(["htg0", "htg1"], ["gtm"])
        P.inherit(["junkws"], ["ofm"])
        P.inherit(["h2"], ["hbuf"])
        for tg in range(NTG):
            sl = htg_ap(tg % 2)
            layernorm_tm(f"xtg{tg}", xbuf[:, tg * D_MODEL:(tg + 1) * D_MODEL], sl,
                         f"htg{tg % 2}")
            for c in range(NCH):
                ptile = pt[c % 2]
                P.transp(ptile[:, 0:128], sl[:, c * 128:(c + 1) * 128], cIDN[:],
                         reads=[f"htg{tg % 2}", "cIDN"], writes=[PTN[c % 2]])
                P.cp("vector", hbuf[:, c * 1024 + tg * 128: c * 1024 + (tg + 1) * 128],
                     ptile[:, 0:128], reads=[PTN[c % 2]], writes=["h2"])
        if "h2" in taps:
            mktap("h2", hbuf[:, 0:16384], [128, 16384], BF16, ["h2"])
        P.inherit(["gwsb"], ["utm4", "ktm4", "wo0", "wo1"])
        P.dma("gpsimd", gwsb, wd["gate_w"], "gw", writes=["gwsb"])
        for c in range(NCH):
            ptile = pt[2 + c % 2]
            P.transp(ptile[:, 0:MOE_E], gwsb[:, c * 128:(c + 1) * 128],
                     cIDN[0:MOE_E, 0:MOE_E], reads=["gwsb", "cIDN"],
                     writes=[PTN[2 + c % 2]])
            P.cp("vector", gwt[:, c * MOE_E:(c + 1) * MOE_E], ptile[:, 0:MOE_E],
                 reads=[PTN[2 + c % 2]], writes=["gwt"])
        for tg in range(NTG):
            ps = ptf[2 + tg % 2]
            psn = PTN[2 + tg % 2]
            for kc in range(NCH):
                P.mm(ps[:, 0:MOE_E],
                     hbuf[:, kc * 1024 + tg * 128: kc * 1024 + (tg + 1) * 128],
                     gwt[:, kc * MOE_E:(kc + 1) * MOE_E],
                     start=(kc == 0), stop=(kc == NCH - 1),
                     reads=["h2", "gwt"], writes=[psn])
            P.red(fws[:, 32:33], ps[:, 0:MOE_E], OP.max, negate=True,
                  reads=[psn], writes=["fws"])
            P.act(fws[:, 0:MOE_E], ps[:, 0:MOE_E], AF.Exp, bias=fws[:, 32:33],
                  reads=[psn, "fws"], writes=["fws"])
            P.cp("vector", fws[:, 16:32], fws[:, 0:16], reads=["fws"], writes=["fws"])
            for _it in range(3):
                P.red(fws[:, 33:34], fws[:, 16:32], OP.max, reads=["fws"],
                      writes=["fws"])
                P.ts("vector", fws[:, 34:50], fws[:, 16:32], fws[:, 33:34], OP.is_ge,
                     reads=["fws"], writes=["fws"])
                P.ts("vector", fws[:, 34:50], fws[:, 34:50], -1e30, OP.mult,
                     reads=["fws"], writes=["fws"])
                P.tt("vector", fws[:, 16:32], fws[:, 34:50], fws[:, 16:32],
                     OP.add, reads=["fws"], writes=["fws"])
            P.red(fws[:, 33:34], fws[:, 16:32], OP.max, reads=["fws"], writes=["fws"])
            P.ts("vector", fws[:, 34:50], fws[:, 0:16], fws[:, 33:34], OP.is_ge,
                 reads=["fws"], writes=["fws"])
            P.tt("vector", fws[:, 34:50], fws[:, 34:50], fws[:, 0:16], OP.mult,
                 reads=["fws"], writes=["fws"])
            P.red(fws[:, 50:51], fws[:, 34:50], OP.add, reads=["fws"], writes=["fws"])
            P.recip(fws[:, 50:51], reads=["fws"], writes=["fws"])
            P.ts("vector", combine[:, tg * MOE_E:(tg + 1) * MOE_E], fws[:, 34:50],
                 fws[:, 50:51], OP.mult, reads=["fws"], writes=["combine"])
        if "combine" in taps:
            mktap("combine", combine[:], [128, NTG * MOE_E], F32, ["combine"])

        WALL = [WN[h] for h in range(HPG)]
        P.inherit([f"abuf{i}" for i in range(MOE_IC)] + ["aws0", "aws1"],
                  ["q4", "k4", "utm4", "ktm4", "vtm4", "gtm", "htg0", "htg1",
                   "wo0", "wo1", "ofm"] + WALL)
        P.inherit([f"wdb{i}" for i in range(MOE_IC)],
                  ["utm4", "ktm4", "vtm4", "gtm", "sgate", "gwsb",
                   "wslA", "hq0", "hq1",
                   "wf2sl", "wg2sl", "bg2sl", "htg0", "htg1",
                   "wo0", "wo1", "ofm", "junkws"] + WALL)
        P.inherit(["wgu0", "wgu1", "wgu2", "wgu3"], ["ofm", "junkws"] + WALL)

        experts = [("sh", 0, None), ("sh", 1, None)] + \
                  [("ex", 0, e) for e in range(MOE_E)]
        gsct = [0]

        def moe_expert(kind, half, e):
            IC = MOE_IC
            if kind == "sh":
                wg_src, wu_src, wdn_src = wd["sw_gate"], wd["sw_up"], wd["sw_down"]
                ic_off = half * MOE_IC
            else:
                wg_src, wu_src = wd["w_gate"][e], wd["w_up"][e]
                wdn_src = wd["w_down"][e]
                ic_off = 0
            for ic in range(IC):
                sA = gsct[0] % 4
                sB = (gsct[0] + 1) % 4
                gsct[0] += 2
                slA = sar[:, FF_WGU + sA * 2048: FF_WGU + (sA + 1) * 2048]
                slB = sar[:, FF_WGU + sB * 2048: FF_WGU + (sB + 1) * 2048]
                P.dma("gpsimd", slA.rearrange("p (c i) -> p c i", c=NCH),
                      wg_src[:, (ic_off + ic) * 128:(ic_off + ic + 1) * 128]
                      .rearrange("(c p) i -> p c i", p=128),
                      f"wgu{sA}", writes=[f"wgu{sA}"])
                P.dma("gpsimd", slB.rearrange("p (c i) -> p c i", c=NCH),
                      wu_src[:, (ic_off + ic) * 128:(ic_off + ic + 1) * 128]
                      .rearrange("(c p) i -> p c i", p=128),
                      f"wgu{sB}", writes=[f"wgu{sB}"])
                for tt in range(2):
                    gps, gpsn = ptf[0], PTN[0]
                    ups, upsn = ptf[1], PTN[1]
                    for kc in range(NCH):
                        P.mm(gps[:, 0:512], slA[:, kc * 128:(kc + 1) * 128],
                             hbuf[:, kc * 1024 + tt * 512: kc * 1024 + (tt + 1) * 512],
                             start=(kc == 0), stop=(kc == NCH - 1),
                             reads=[f"wgu{sA}", "h2"], writes=[gpsn])
                    for kc in range(NCH):
                        P.mm(ups[:, 0:512], slB[:, kc * 128:(kc + 1) * 128],
                             hbuf[:, kc * 1024 + tt * 512: kc * 1024 + (tt + 1) * 512],
                             start=(kc == 0), stop=(kc == NCH - 1),
                             reads=[f"wgu{sB}", "h2"], writes=[upsn])
                    asl = sar[:, FF_WS + tt * 512: FF_WS + (tt + 1) * 512]
                    P.act(asl, gps[:, 0:512], AF.Silu, reads=[gpsn],
                          writes=[f"aws{tt}"])
                    P.tt("vector",
                         sar[:, FF_A + ic * 1024 + tt * 512:
                             FF_A + ic * 1024 + (tt + 1) * 512],
                         asl, ups[:, 0:512], OP.mult,
                         reads=[f"aws{tt}", upsn], writes=[f"abuf{ic}"])
            for ic in range(IC):
                P.dma("gpsimd", sar[:, FF_WD + ic * 2048: FF_WD + (ic + 1) * 2048],
                      wdn_src[(ic_off + ic) * 128:(ic_off + ic + 1) * 128, :],
                      f"wdb{ic}", writes=[f"wdb{ic}"])
            for tg in range(NTG):
                for ic in range(IC):
                    for dt_ in range(4):
                        P.mm(pb[dt_][:, 0:512],
                             sar[:, FF_A + ic * 1024 + tg * 128:
                                 FF_A + ic * 1024 + (tg + 1) * 128],
                             sar[:, FF_WD + ic * 2048 + dt_ * 512:
                                 FF_WD + ic * 2048 + (dt_ + 1) * 512],
                             start=(ic == 0), stop=(ic == IC - 1),
                             reads=[f"abuf{ic}", f"wdb{ic}"], writes=[PBN[dt_]])
                for dt_ in range(4):
                    xsl = xbuf[:, tg * D_MODEL + dt_ * 512:
                               tg * D_MODEL + (dt_ + 1) * 512]
                    if kind == "sh":
                        P.tt("vector", xsl, pb[dt_][:, 0:512], xsl, OP.add,
                             reads=[PBN[dt_]], writes=[f"xtg{tg}"])
                    else:
                        P.stt("vector", xsl, pb[dt_][:, 0:512],
                              combine[:, tg * MOE_E + e: tg * MOE_E + e + 1], xsl,
                              OP.mult, OP.add,
                              reads=[PBN[dt_], "combine"], writes=[f"xtg{tg}"])

        for kind, half, e in experts:
            moe_expert(kind, half, e)

        P.dma("sync", dout.rearrange("(g p) d -> p g d", p=128),
              xbuf[:].rearrange("p (g d) -> p g d", g=NTG), "outd",
              reads=[f"xtg{g}" for g in range(NTG)])
        P.final_tags.append("outd")

        prog.emit()

    return nc, tap_outs


def _consts():
    i = np.arange(C)
    lt2 = ((i[:, None] <= i[None, :]).astype(np.float32)
           - (i <= C // 2 - 1).astype(np.float32)[:, None])
    LT2P = np.zeros((128, 128), np.float32)
    LT2P[0:64, 0:64] = lt2
    LT2P[64:128, 64:128] = lt2
    MIDS = np.zeros((128, 4), np.float32)
    MIDS[0:64, 0] = (i <= C // 2 - 1)
    MIDS[0:64, 1] = (i > C // 2 - 1)
    MIDS[64:128, 2] = (i <= C // 2 - 1)
    MIDS[64:128, 3] = (i > C // 2 - 1)
    MASKP = np.zeros((128, 128), np.float32)
    mut = (i[:, None] <= i[None, :]).astype(np.float32)
    MASKP[0:64, 0:64] = mut
    MASKP[64:128, 64:128] = mut
    cfv = np.zeros((128, 8), np.float32)
    cfv[:, 0] = EPS
    cfv[:, 1] = 1e-6
    cfv[:, 2] = LNSC
    cfv[:, 3] = -1.0
    cfv[:, 4] = 1.0 / DV
    cfv[0:64, 5] = 1.0
    cfv[64:128, 6] = 1.0
    bf = ml_dtypes.bfloat16
    return {
        "cF32": cfv,
        "cIDN": np.eye(128, dtype=bf),
        "cLT2P": LT2P.astype(bf),
        "cMIDS": MIDS.astype(bf),
        "cMASKP": MASKP.astype(bf),
        "cONES1": np.ones((1, 128), bf),
    }


def prep_core_inputs(inputs):
    x = np.ascontiguousarray(np.asarray(inputs["L0_attn_hidden_states"],
                                        np.float32).reshape(8192, D_MODEL))
    shared = dict(_consts())
    ren = {"Wq": "L0_attn_Wq", "Wk": "L0_attn_Wk", "Wv": "L0_attn_Wv",
           "Ww": "L0_attn_Ww", "cq": "L0_attn_cq", "ck": "L0_attn_ck",
           "cv": "L0_attn_cv", "Wf1": "L0_attn_Wf1", "Wf2": "L0_attn_Wf2",
           "Wb": "L0_attn_Wb", "Wg1": "L0_attn_Wg1", "Wg2": "L0_attn_Wg2",
           "Wo": "L0_attn_Wo", "gate_w": "L0_ffn_gate_w",
           "w_gate": "L0_ffn_w_gate", "w_up": "L0_ffn_w_up",
           "w_down": "L0_ffn_w_down", "sw_gate": "L0_ffn_sw_gate",
           "sw_up": "L0_ffn_sw_up", "sw_down": "L0_ffn_sw_down"}
    for k, src in ren.items():
        shared[k] = np.ascontiguousarray(np.asarray(inputs[src], np.float32))
    shared["bg2"] = np.asarray(inputs["L0_attn_bg2"], np.float32).reshape(1, D_MODEL)
    shared["norm_w"] = np.asarray(inputs["L0_attn_norm_w"],
                                  np.float32).reshape(128, 1)
    maps = []
    for c in range(8):
        m = dict(shared)
        r0 = c * T_CORE
        m["xloc"] = x[r0:r0 + T_CORE]
        if c % 4 == 0:
            m["xhalo"] = np.zeros((3, D_MODEL), np.float32)
        else:
            m["xhalo"] = np.ascontiguousarray(x[r0 - 3:r0])
        sm = np.zeros((128, 8), np.float32)
        sm[:, 0:(c % 4)] = 1.0
        sm[:, 4:8] = 1.0 - sm[:, 0:4]
        m["cSMASK"] = sm
        maps.append(m)
    return maps


def kernel(**inputs):
    maps = prep_core_inputs(inputs)
    nc, _ = build_program()
    res = run_bass_kernel_spmd(nc, maps, core_ids=list(range(8)))
    out = np.empty((8192, D_MODEL), np.float32)
    for c in range(8):
        out[c * T_CORE:(c + 1) * T_CORE] = res.results[c]["out"]
    return out.reshape(2, 4096, D_MODEL)
